# version 08
# baseline (speedup 1.0000x reference)
import numpy as np
import ml_dtypes
import concourse.bass as bass
import concourse.mybir as mybir
from concourse.bass_utils import run_bass_kernel_spmd
from contextlib import ExitStack

F32 = mybir.dt.float32
BF16 = mybir.dt.bfloat16
AF = mybir.ActivationFunctionType
ALU = mybir.AluOpType
BF = ml_dtypes.bfloat16

NCORE = 8
TC = 2048
SEG = 512
NSEG = 4
NTILE = 16
CH = 32
BAND = 7
QOFF = 44.0
THRESH = 126.0
BIGNEG = 1e9
D = 1024
H = 16
HD = 64
DH = 4096
PADC = 40
PAD0 = 8
EPS = 1e-5
SEGW = 8 * 16 * PADC
ENGS = ("sync", "gpsimd", "vector", "scalar", "tensor")
DMASEMS = {"sync": ("ldc", "st", "ltA", "ltB"),
           "gpsimd": ("gx", "gc", "lwA", "lwB", "fwA", "fwB"),
           "scalar": ("scA", "scB"), "vector": (), "tensor": ()}


class Prog:
    def __init__(self, nc, ctx):
        self.nc = nc
        self.ops = {e: [] for e in ENGS}
        self.semh = {}
        self.semc = {}
        self.ctx = ctx
        self.trace = []
        self.last_pe_key = None
        self._pe_key_hold = None
        self.phases = []

    def phase(self, name):
        self.phases.append((name, {e: len(self.ops[e]) for e in ENGS}))

    def sem(self, name):
        self.semh[name] = self.ctx.enter_context(self.nc.semaphore(name))
        self.semc[name] = 0

    def pe_mm(self, make, key, sem=None, inc=1):
        if self.last_pe_key != key:
            self.ops["tensor"].append(lambda e: e.nop())
        self.last_pe_key = key
        self.do("tensor", make, sem, inc)
        self.last_pe_key = key

    def do(self, eng, make, sem=None, inc=1, fence=None):
        if eng == "tensor":
            self.last_pe_key = None
        if fence is None:
            fence = eng in ("vector", "scalar", "gpsimd")
        if sem is None:
            if fence:
                self.ops[eng].append(
                    lambda e, mk=make: (mk(e), e.drain()))
            else:
                self.ops[eng].append(make)
        else:
            self.semc[sem] += inc
            self.trace.append((eng, "op+inc", sem, self.semc[sem]))
            h = self.semh[sem]
            if fence:
                self.ops[eng].append(
                    lambda e, mk=make, hh=h, ii=inc: (
                        mk(e).then_inc(hh, ii), e.drain()))
            else:
                self.ops[eng].append(
                    lambda e, mk=make, hh=h, ii=inc: mk(e).then_inc(hh, ii))

    def dma(self, eng, out, in_, sem, slow=False):
        self.semc[sem] += 16
        self.trace.append((eng, "dma+inc", sem, self.semc[sem]))
        h = self.semh[sem]
        self.ops[eng].append(
            lambda e, o=out, i=in_, hh=h, sl=slow: e.dma_start(
                out=o, in_=i, allow_slow_non_contiguous=sl).then_inc(hh, 16))
        return (sem, self.semc[sem])

    def wait(self, eng, sem, val=None):
        v = self.semc[sem] if val is None else val
        h = self.semh[sem]
        self.trace.append((eng, "wait", sem, v))
        self.ops[eng].append(lambda e, hh=h, vv=v: e.wait_ge(hh, vv))

    def waits(self, eng, recs):
        seen = {}
        for sem, v in recs:
            seen[sem] = max(seen.get(sem, -1), v)
        for sem, v in seen.items():
            if v > 0:
                self.wait(eng, sem, v)

    def rec(self, sem):
        return (sem, self.semc[sem])

    def val(self, sem):
        return self.semc[sem]

    def barrier(self):
        esem = {"sync": "bsy", "gpsimd": "bgp", "vector": "bve",
                "scalar": "bsc", "tensor": "bpe"}
        for e in ENGS:
            for ds in DMASEMS[e]:
                self.wait(e, ds)
            self.semc[esem[e]] += 1
            self.trace.append((e, "barrier-inc", esem[e], self.semc[esem[e]]))
            h = self.semh[esem[e]]
            self.ops[e].append(lambda x, hh=h: x.nop().then_inc(hh, 1))
        for e in ENGS:
            for o in ENGS:
                if o != e:
                    self.wait(e, esem[o])


class Arena:
    def __init__(self, nc, ctx, name, nwords):
        self.t = ctx.enter_context(nc.sbuf_tensor(name, [128, nwords], F32))
        self.nwords = nwords
        self.off = 0
        self.peak = 0

    def _alloc(self, n, align=16):
        o = (self.off + align - 1) // align * align
        self.off = o + n
        self.peak = max(self.peak, self.off)
        assert self.off <= self.nwords, f"arena overflow: {self.off} > {self.nwords}"
        return o

    def _shape(self, ap, shape):
        if len(shape) > 1:
            names = [f"d{i}" for i in range(len(shape))]
            ap = ap.rearrange("p (" + " ".join(names) + ") -> p " + " ".join(names),
                              **{n: s for n, s in zip(names, shape)})
        return ap

    def f32(self, *shape):
        n = int(np.prod(shape))
        o = self._alloc(n)
        return self._shape(self.t.ap()[:, o:o + n], shape)

    def bf16(self, *shape):
        n = int(np.prod(shape))
        nw = (n + 1) // 2
        o = self._alloc(nw)
        return self._shape(self.t.ap()[:, o:o + nw].bitcast(BF16), shape)

    def mark(self):
        return self.off

    def reset(self, m):
        self.off = m


def build(dbg=(), stop_after=None, nseg=NSEG):
    nc = bass.Bass("TRN2", target_bir_lowering=False, detect_race_conditions=False)

    def din(name, shape, dt=F32):
        return nc.dram_tensor(name, list(shape), dt, kind="ExternalInput").ap()

    x_hbm = din("x_hbm", (TC, D))
    xprev = din("xprev", (1, D))
    wtt = din("wtt", (4, 128, 8 * D))
    oWtt = din("oWtt", (128, 8 * D))
    laT = din("laT", (D, 384))
    lbT = din("lbT", (128, 3, D))
    ll_cols = din("ll_cols", (128, 5, 8))
    xw_col = din("xw_col", (128, 8))
    dll_col = din("dll_col", (128, 8))
    u_col = din("u_col", (128, 8))
    S0_in = din("S0", (128, 512))
    pm_in = din("pm", (128, 8))
    pm1_in = din("pm1", (128, 8))
    ident_in = din("ident", (128, 128), BF16)
    mscale_in = din("mscaleA", (128, 512))
    bmask_in = din("bmask", (128, SEG))
    ind16_in = din("ind16", (128, 8, 16), BF16)
    lnw_in = din("lnw_tile", (128, D))
    lnb_in = din("lnb_tile", (128, D))
    w3s_in = din("w3s_tile", (128, D))
    scols_in = din("s_cols", (128, 4, 32))
    w12t = din("w12t", (32, 128, 16 * 128))
    h1tt = din("h1tt", (32, 128, 32 * 128))
    h2tt = din("h2tt", (32, 128, 32 * 128))
    w3tt = din("w3tt", (4, 128, 32 * 256))

    out_hbm = nc.dram_tensor("out", [TC, D], F32, kind="ExternalOutput").ap()

    r_d = nc.dram_tensor("r_d", [NSEG, 4, 128, SEGW // 4], BF16,
                         kind="Internal").ap()
    k_d = nc.dram_tensor("k_d", [NSEG, 4, 128, SEGW // 4], BF16,
                         kind="Internal").ap()
    c_d = nc.dram_tensor("c_d", [NSEG, 4, 128, SEGW // 4], F32,
                         kind="Internal").ap()
    v_d = nc.dram_tensor("v_d", [NTILE, 128, D], BF16, kind="Internal").ap()
    g_d = nc.dram_tensor("g_d", [NTILE, 128, D], BF16, kind="Internal").ap()
    M_d = nc.dram_tensor("M_d", [64, 128, 512], F32, kind="Internal").ap()
    x1_d = nc.dram_tensor("x1_d", [NTILE, 128, D], F32, kind="Internal").ap()
    o3_d = nc.dram_tensor("o3_d", [NTILE, 128, D], F32, kind="Internal").ap()
    ha_d = nc.dram_tensor("ha_d", [32, 128, TC], BF16, kind="Internal").ap()
    hb_d = nc.dram_tensor("hb_d", [32, 128, TC], BF16, kind="Internal").ap()
    ab_scr = nc.dram_tensor("ab_scr", [2, 128 * 128 * 16], BF16, kind="Internal").ap()
    cc_in = nc.dram_tensor("cc_in", [128, 520], F32, kind="Internal").ap()
    cc_out = nc.dram_tensor("cc_out", [128 * NCORE, 520], F32, kind="Internal",
                            addr_space="Shared").ap()

    dbg_out = {}
    for name, shape, dt in dbg:
        dbg_out[name] = nc.dram_tensor("dbg_" + name, list(shape), dt,
                                       kind="ExternalOutput").ap()

    with ExitStack() as ctx:
        ar = Arena(nc, ctx, "arena", 53200)
        pall = ctx.enter_context(nc.psum_tensor("pall", [128, 4096], F32))

        def pbank(i, n=512):
            return pall[:, 512 * i:512 * i + n]

        def pbank_bf(i, n):
            return pall[:, 512 * i:512 * i + (n + 1) // 2].bitcast(BF16)

        P = Prog(nc, ctx)
        globals()["LAST_PROG"] = P
        for s in ("ldc", "gx", "gc", "lwA", "lwB", "ltA", "ltB", "scA", "scB",
                  "fwA", "fwB", "pe", "ve", "sc", "gp", "st", "cc", "outst",
                  "bsy", "bgp", "bve", "bsc", "bpe"):
            P.sem(s)

        def tap(name, ap):
            if name in dbg_out:
                P.barrier()
                rec = P.dma("sync", dbg_out[name], ap, "st")
                P.wait("sync", *rec)
                P.barrier()

        P.phase("init")
        ident_sb = ar.bf16(128)
        mscale_sb = ar.f32(512)
        bmask_sb = ar.f32(SEG)
        ind16_sb = ar.bf16(8, 16)
        llc_sb = ar.f32(5, 8)
        xwc_sb = ar.f32(8)
        dllc_sb = ar.f32(8)
        uc_sb = ar.f32(8)
        pm_sb = ar.f32(8)
        pm1_sb = ar.f32(8)
        Spp = ar.f32(2, 512)
        Srun_sb = ar.f32(512)
        Drun_sb = ar.f32(8)
        D_all = ar.f32(64, 8)
        cend_all = ar.f32(64, 8)
        mr_all = ar.f32(64, 8)
        mk_all = ar.f32(64, 8)
        lnw_sb = ar.f32(D)
        lnb_sb = ar.f32(D)
        w3s_sb = ar.f32(D)
        scol_sb = ar.f32(4, 32)
        mb_sb = ar.f32(16, 8)
        PERSIST = ar.mark()

        for dst, src in ((ident_sb, ident_in), (mscale_sb, mscale_in),
                         (bmask_sb, bmask_in), (ind16_sb, ind16_in),
                         (llc_sb, ll_cols), (xwc_sb, xw_col), (dllc_sb, dll_col),
                         (uc_sb, u_col), (pm_sb, pm_in), (pm1_sb, pm1_in),
                         (Spp[:, 0], S0_in), (lnw_sb, lnw_in), (lnb_sb, lnb_in),
                         (w3s_sb, w3s_in), (scol_sb, scols_in)):
            P.dma("sync", dst, src, "ldc")
        P.do("vector", lambda v: v.memset(Srun_sb, 0.0), None)
        P.do("vector", lambda v: v.memset(Drun_sb, 1.0), None)

        ATW0 = ar.mark()
        wts = [ar.bf16(8, D), ar.bf16(8, D)]
        la_sb = ar.bf16(8, 384)
        lb_sb = ar.bf16(3, D)
        P.dma("gpsimd", la_sb, laT.rearrange("(m p) o -> p m o", p=128), "gc")
        P.dma("gpsimd", lb_sb, lbT, "gc")
        wt_free = {0: ("pe", 0), 1: ("pe", 0)}
        wsem = {0: "lwA", 1: "lwB"}

        x_bf = ar.bf16(4, D)
        x_fm = ar.bf16(8, 513)
        dx_fm = ar.bf16(8, SEG)
        t1_fm = ar.bf16(8, SEG)
        inp_fm = ar.bf16(8, SEG)
        zt_sb = ar.bf16(3, SEG)
        what_sb = ar.f32(SEG)
        cscr_sb = ar.f32(SEG)
        chat_sb = ar.f32(8, 16, PADC)
        r_fm = ar.bf16(8, 16, PADC)
        k_fm = ar.bf16(8, 16, PADC)
        v_tm = ar.bf16(4, D)
        g_tm = ar.bf16(4, D)
        ek_sb = ar.bf16(16, 32)
        ktr_scr = ar.bf16(2, 128)
        ktm_sb = ar.bf16(4, D)
        M_sb = ar.f32(2, 512)
        P1_END = ar.mark()

        P.dma("gpsimd", x_fm[:, :, 0:1],
              xprev.rearrange("o (m p) -> p m o", p=128), "gc", slow=True)
        P.do("vector", lambda v: v.memset(
            x_bf.rearrange("p a d -> p (a d)")[:, 0:2048], 0.0), "ve")
        ZED = P.rec("ve")
        P.waits("gpsimd", [ZED])
        for i in range(2):
            P.dma("gpsimd", ab_scr[i].rearrange("(p x) -> p x", p=128),
                  x_bf.rearrange("p a d -> p (a d)")[:, 0:2048], "gc")

        P.barrier()

        PORD = (4, 1, 2, 0, 3)
        proj_col = {0: 0, 1: D, 2: 2 * D, 3: 3 * D}

        for s in range(nseg):
            P.phase(f"p1s{s}")
            if s > 0:
                P.do("vector", lambda v: v.tensor_copy(
                    x_fm[:, :, 0:1], x_fm[:, :, 512:513]), "ve")
            BCOL = P.rec("ve")
            XBF = P.dma("gpsimd", x_bf,
                        x_hbm[s * SEG:(s + 1) * SEG, :].rearrange(
                            "(a p) d -> p a d", p=128), "gx")
            P.do("vector", lambda v: v.memset(
                chat_sb.rearrange("p m c w -> p (m c w)"), 0.0), None)
            P.do("vector", lambda v: v.memset(
                k_fm.rearrange("p m c w -> p (m c w)"), 0.0), None)
            P.do("vector", lambda v: v.memset(
                r_fm.rearrange("p m c w -> p (m c w)"), 0.0), "ve")
            PADZ = P.rec("ve")

            P.wait("tensor", "gx")
            P.wait("tensor", "gc")
            P.wait("tensor", "ldc")
            P.waits("tensor", [BCOL])
            ev_at = {}
            for i, (a, m) in enumerate([(a, m) for a in range(4) for m in range(8)]):
                b = 2 + (i % 2)
                if i >= 2:
                    P.waits("tensor", [ev_at[i - 2]])
                P.do("tensor", lambda t, a=a, m=m, b=b: t.transpose(
                    pbank_bf(b, 128)[:, 0:128],
                    x_bf[:, a, 128 * m:128 * (m + 1)], ident_sb), "pe")
                PEh = P.rec("pe")
                P.waits("vector", [PEh])
                P.do("vector", lambda v, a=a, m=m, b=b: v.tensor_copy(
                    x_fm[:, m, 1 + 128 * a:1 + 128 * (a + 1)],
                    pbank_bf(b, 128)[:, 0:128]), "ve")
                ev_at[i] = P.rec("ve")
            P.do("vector", lambda v: v.tensor_sub(
                dx_fm, x_fm[:, :, 0:512], x_fm[:, :, 1:513]), None)
            for m in range(8):
                P.do("vector", lambda v, m=m: v.scalar_tensor_tensor(
                    t1_fm[:, m, :], dx_fm[:, m, :], xwc_sb[:, m:m + 1],
                    x_fm[:, m, 1:513], ALU.mult, ALU.add), None)
            P.do("vector", lambda v: v.nop(), "ve")
            LERP = P.rec("ve")
            if stop_after == "xprep":
                P.waits("sync", [LERP])
                tap("xfm0", x_fm.rearrange("p m t -> p (m t)"))
                tap("dx0", dx_fm.rearrange("p m t -> p (m t)"))
                tap("lerp0", t1_fm.rearrange("p m t -> p (m t)"))
                _close(P, nc, ctx, [])
                return nc
            if s == 0:
                P.waits("sync", [LERP])
                tap("xfm0", x_fm.rearrange("p m t -> p (m t)"))
                tap("dx0", dx_fm.rearrange("p m t -> p (m t)"))
                tap("lerp0", t1_fm.rearrange("p m t -> p (m t)"))

            P.waits("tensor", [LERP])
            zt_done = {}
            for q in range(3):
                b = q % 2
                if q >= 2:
                    P.waits("tensor", [zt_done[q - 2]])
                for kk in range(8):
                    P.do("tensor", lambda t, q=q, kk=kk, b=b: t.matmul(
                        pbank(b), la_sb[:, kk, 128 * q:128 * (q + 1)],
                        t1_fm[:, kk, :], start=(kk == 0), stop=(kk == 7)), "pe")
                PEq = P.rec("pe")
                P.waits("scalar", [PEq])
                P.do("scalar", lambda e, q=q, b=b: e.activation(
                    zt_sb[:, q, :], pbank(b), AF.Tanh), "sc")
                zt_done[q] = P.rec("sc")
            if s == 0:
                P.waits("sync", [zt_done[2]])
                tap("zt0", zt_sb.rearrange("p q t -> p (q t)"))
            if stop_after == "lora":
                _close(P, nc, ctx, [])
                return nc

            last_evac = {0: zt_done[2], 1: zt_done[1], 4: ("sc", 0), 5: ("sc", 0)}
            scan_done = ("ve", 0)
            inp_done = ("gp", 0)
            kstore = None
            vprj = ("sc", 0)
            DDONE = ("sc", 0)
            slab_of = {1: 0, 2: 1, 0: 0, 3: 1}

            def wt_load(pi):
                sl = slab_of[pi]
                P.waits("gpsimd", [wt_free[sl]])
                P.dma("gpsimd", wts[sl].rearrange("p m o -> p (m o)"),
                      wtt[pi], wsem[sl])

            for pi in (1, 2):
                wt_load(pi)

            for pi in PORD:
                q, base = pi // 2, 64 * (pi % 2)
                P.waits("tensor", [zt_done[q], inp_done])
                for m in range(8):
                    b = m % 2
                    P.waits("tensor", [last_evac[b]])
                    P.do("tensor", lambda t, m=m, b=b, q=q, base=base: t.matmul(
                        pbank(b), lb_sb[base:base + 64, q, 128 * m:128 * (m + 1)],
                        zt_sb[base:base + 64, q, :], start=True, stop=True), "pe")
                    PEm = P.rec("pe")
                    P.waits("vector", [PEm])
                    P.do("vector", lambda v, m=m, b=b, pi=pi: v.scalar_tensor_tensor(
                        t1_fm[:, m, :], pbank(b), llc_sb[:, pi, m:m + 1],
                        dx_fm[:, m, :], ALU.add, ALU.mult), "ve")
                    last_evac[b] = P.rec("ve")
                    P.waits("gpsimd", [last_evac[b]])
                    P.do("gpsimd", lambda g, m=m: g.tensor_add(
                        inp_fm[:, m, :], t1_fm[:, m, :], x_fm[:, m, 1:513]), "gp")
                INP = P.rec("gp")
                inp_done = INP
                if s == 0 and pi == 4:
                    P.waits("sync", [INP])
                    tap("inpd0", inp_fm.rearrange("p m t -> p (m t)"))

                if pi == 4:
                    P.waits("tensor", [INP, ev_at[31]])
                    for kk in range(8):
                        P.do("tensor", lambda t, kk=kk: t.matmul(
                            pbank(2)[0:64, :], la_sb[:, kk, 256:320],
                            inp_fm[:, kk, :], start=(kk == 0), stop=(kk == 7)), "pe")
                    PEzd = P.rec("pe")
                    P.waits("scalar", [PEzd])
                    P.do("scalar", lambda e: e.activation(
                        zt_sb[0:64, 2, :], pbank(2)[0:64, :], AF.Tanh), "sc")
                    ZD = P.rec("sc")
                    P.waits("tensor", [ZD])
                    for m in range(8):
                        b = m % 2
                        P.waits("tensor", [last_evac[b]])
                        P.do("tensor", lambda t, m=m, b=b: t.matmul(
                            pbank(b), lb_sb[0:64, 2, 128 * m:128 * (m + 1)],
                            zt_sb[0:64, 2, :], start=True, stop=True), "pe")
                        PEd = P.rec("pe")
                        P.waits("scalar", [PEd, scan_done, PADZ])
                        P.do("scalar", lambda e, m=m, b=b: e.activation(
                            what_sb, pbank(b), AF.Exp, bias=dllc_sb[:, m:m + 1]),
                            "sc")
                        last_evac[b] = P.rec("sc")
                        P.waits("vector", [last_evac[b]])
                        P.do("vector", lambda v: v.tensor_tensor_scan(
                            cscr_sb, bmask_sb, what_sb,
                            0.0, ALU.mult, ALU.add), None)
                        P.do("vector", lambda v, m=m: v.tensor_copy(
                            chat_sb[:, m, :, PAD0:PADC],
                            cscr_sb.rearrange("p (c w) -> p c w", w=32)), "ve")
                        scan_done = P.rec("ve")
                    P.do("vector", lambda v, s=s: v.tensor_copy(
                        cend_all[:, 16 * s:16 * s + 16, :],
                        chat_sb[:, :, :, PADC - 1].transpose([0, 2, 1])), "ve")
                    CEND = P.rec("ve")
                    P.waits("scalar", [CEND])
                    P.do("scalar", lambda e, s=s: e.activation(
                        D_all[:, 16 * s:16 * s + 16, :],
                        cend_all[:, 16 * s:16 * s + 16, :], AF.Exp, scale=-1.0), "sc")
                    DDONE = P.rec("sc")
                    ce = cend_all[:, 16 * s:16 * s + 16, :].rearrange(
                        "p c m -> p (c m)")
                    mbv = mb_sb.rearrange("p c m -> p (c m)")
                    mrv = mr_all[:, 16 * s:16 * s + 16, :].rearrange(
                        "p c m -> p (c m)")
                    mkv = mk_all[:, 16 * s:16 * s + 16, :].rearrange(
                        "p c m -> p (c m)")
                    P.do("vector", lambda v, ce=ce, mbv=mbv: v.tensor_scalar(
                        mbv, ce, THRESH, None, ALU.is_gt), None)
                    P.do("vector", lambda v, ce=ce, mrv=mrv: v.tensor_scalar(
                        mrv, ce, 0.5, -QOFF / 2, ALU.mult, ALU.add), None)
                    P.do("vector", lambda v, mbv=mbv, mrv=mrv: v.scalar_tensor_tensor(
                        mrv, mbv, -BIGNEG, mrv, ALU.mult, ALU.add), None)
                    P.do("vector", lambda v, ce=ce, mkv=mkv: v.tensor_scalar(
                        mkv, ce, 0.5, QOFF / 2, ALU.mult, ALU.add), None)
                    P.do("vector", lambda v, mbv=mbv, mkv=mkv: v.scalar_tensor_tensor(
                        mkv, mbv, BIGNEG, mkv, ALU.mult, ALU.add), "ve")
                    if s == 0:
                        P.waits("sync", [scan_done, P.rec("ve"), DDONE])
                        tap("chat0", chat_sb.rearrange("p m c w -> p (m c w)"))
                    if stop_after == "dpath":
                        _close(P, nc, ctx, [])
                        return nc
                    continue

                if pi in (1, 0):
                    P.wait("tensor", wsem[slab_of[pi]])
                    dst = k_fm if pi == 1 else r_fm
                    for m in range(8):
                        b = m % 2
                        P.waits("tensor", [last_evac[b], INP, PADZ])
                        for kk in range(8):
                            P.do("tensor", lambda t, m=m, kk=kk, b=b,
                                 sl=slab_of[pi]: t.matmul(
                                pbank(b),
                                wts[sl][:, kk, 128 * m:128 * (m + 1)],
                                inp_fm[:, kk, :], start=(kk == 0), stop=(kk == 7)),
                                "pe")
                        PEp = P.rec("pe")
                        P.waits("scalar", [PEp])
                        P.do("scalar", lambda e, m=m, b=b, dst=dst: e.copy(
                            dst[:, m, :, PAD0:PADC],
                            pbank(b).rearrange("p (c w) -> p c w", w=32)), "sc")
                        last_evac[b] = P.rec("sc")
                    wt_free[slab_of[pi]] = P.rec("pe")
                    nxt = {1: 0, 2: 3}.get(pi)
                    if nxt is not None:
                        wt_load(nxt)
                    PRJ = P.rec("sc")
                    P.waits("sync", [PRJ])
                    dd = k_d if pi == 1 else r_d
                    for al in range(4):
                        strec = P.dma("sync",
                                      dd[s, al].rearrange(
                                          "p (m c w) -> p m c w", m=8, w=PADC),
                                      dst[:, :, 4 * al:4 * al + 4, :], "st")
                    if pi == 1:
                        kstore = strec
                    if s == 0:
                        tap("kfm0" if pi == 1 else "rfm0",
                            dst.rearrange("p m c w -> p (m c w)"))
                else:
                    P.wait("tensor", wsem[slab_of[pi]])
                    dst = v_tm if pi == 2 else g_tm
                    for a in range(4):
                        for n in range(2):
                            b = 4 + n
                            P.waits("tensor", [last_evac[b], INP])
                            for kk in range(8):
                                P.do("tensor",
                                     lambda t, a=a, n=n, kk=kk, b=b,
                                     sl=slab_of[pi]: t.matmul(
                                         pbank(b),
                                         inp_fm[:, kk, 128 * a:128 * (a + 1)],
                                         wts[sl][:, kk, 512 * n:512 * (n + 1)],
                                         start=(kk == 0), stop=(kk == 7)), "pe")
                            PEp = P.rec("pe")
                            P.waits("scalar", [PEp])
                            P.do("scalar", lambda e, a=a, n=n, b=b, dst=dst: e.copy(
                                dst[:, a, 512 * n:512 * (n + 1)], pbank(b)), "sc")
                            last_evac[b] = P.rec("sc")
                    wt_free[slab_of[pi]] = P.rec("pe")
                    nxt = {1: 0, 2: 3}.get(pi)
                    if nxt is not None:
                        wt_load(nxt)
                    PRJ = P.rec("sc")
                    if pi == 2:
                        vprj = PRJ
                    P.waits("sync", [PRJ])
                    dd = v_d if pi == 2 else g_d
                    for a in range(4):
                        P.dma("sync", dd[4 * s + a], dst[:, a, :], "st")
                    if s == 0:
                        tap("vtm0" if pi == 2 else "gtm0",
                            dst.rearrange("p a d -> p (a d)"))

            P.waits("sync", [scan_done])
            for al in range(4):
                P.dma("sync",
                      c_d[s, al].rearrange("p (m c w) -> p m c w", m=8, w=PADC),
                      chat_sb[:, :, 4 * al:4 * al + 4, :], "st")

            P.wait("vector", "st")
            for m in range(8):
                P.do("vector", lambda v, m=m: v.tensor_sub(
                    what_sb.rearrange("p (c w) -> p c w", w=32),
                    chat_sb[:, m, :, PAD0:PADC],
                    chat_sb[:, m, :, PADC - 1:PADC].broadcast_to((128, 16, 32))),
                    "ve")
                EARG = P.rec("ve")
                P.waits("scalar", [EARG])
                P.do("scalar", lambda e: e.activation(
                    ek_sb, what_sb.rearrange("p (c w) -> p c w", w=32), AF.Exp),
                    "sc")
                EK = P.rec("sc")
                P.waits("vector", [EK])
                P.do("vector", lambda v, m=m: v.tensor_mul(
                    k_fm[:, m, :, PAD0:PADC], k_fm[:, m, :, PAD0:PADC], ek_sb),
                    "ve")
            KHAT = P.rec("ve")
            if s == 0:
                P.waits("sync", [KHAT])
                tap("khat0", k_fm.rearrange("p m c w -> p (m c w)"))
            if stop_after == "khat":
                _close(P, nc, ctx, [])
                return nc

            tr_done = {}
            stage_done = {}
            for i, (a, m) in enumerate([(a, m) for a in range(4) for m in range(8)]):
                b = 2 + (i % 2)
                j = i % 2
                if i >= 2:
                    P.waits("vector", [stage_done[i - 2]])
                P.do("vector", lambda v, a=a, m=m, j=j: v.tensor_copy(
                    ktr_scr[:, j, :].rearrange("p (c w) -> p c w", w=32),
                    k_fm[:, m, 4 * a:4 * a + 4, PAD0:PADC]), "ve")
                STG = P.rec("ve")
                if i >= 2:
                    P.waits("tensor", [tr_done[i - 2]])
                P.waits("tensor", [STG])
                P.do("tensor", lambda t, j=j, b=b: t.transpose(
                    pbank_bf(b, 128)[:, 0:128],
                    ktr_scr[:, j, :], ident_sb), "pe")
                PEt = P.rec("pe")
                stage_done[i] = PEt
                P.waits("vector", [PEt])
                P.do("vector", lambda v, a=a, m=m, b=b: v.tensor_copy(
                    ktm_sb[:, a, 128 * m:128 * (m + 1)],
                    pbank_bf(b, 128)[:, 0:128]), "ve")
                tr_done[i] = P.rec("ve")
            KTM = P.rec("ve")
            if s == 0:
                P.waits("sync", [KTM])
                tap("ktm0", ktm_sb.rearrange("p a d -> p (a d)"))
            if stop_after == "ktm":
                _close(P, nc, ctx, [])
                return nc

            m_evac = {0: ("sc", 0), 1: ("sc", 0)}
            m_store = {0: ("st", 0), 1: ("st", 0)}
            sum_done = ("gp", 0)
            for cl in range(16):
                a, ccc = cl // 4, cl % 4
                P.waits("tensor", [KTM, vprj, m_evac[cl % 2]])
                for h in range(16):
                    P.do("tensor", lambda t, a=a, ccc=ccc, h=h, pb=6 + cl % 2:
                         t.matmul(
                        pbank(pb)[64 * (h % 2):64 * (h % 2) + 64,
                                  64 * (h // 2):64 * (h // 2) + 64],
                        ktm_sb[32 * ccc:32 * ccc + 32, a,
                               128 * (h // 2) + 64 * (h % 2):
                               128 * (h // 2) + 64 * (h % 2) + 64],
                        v_tm[32 * ccc:32 * ccc + 32, a, 64 * h:64 * h + 64],
                        start=True, stop=True,
                        tile_position=(32 * ccc, 64 * (h % 2))), "pe")
                PEM = P.rec("pe")
                P.waits("scalar", [PEM, sum_done])
                if cl >= 2:
                    P.wait("scalar", "st")
                P.do("scalar", lambda e, cl=cl: e.copy(
                    M_sb[:, cl % 2, :], pbank(6 + cl % 2)), "sc")
                m_evac[cl % 2] = P.rec("sc")
                P.waits("sync", [m_evac[cl % 2]])
                m_store[cl % 2] = P.dma("sync", M_d[16 * s + cl],
                                        M_sb[:, cl % 2, :], "st")
                P.waits("gpsimd", [m_evac[cl % 2], DDONE])
                dcol = D_all[:, 16 * s + cl, :].unsqueeze(2).broadcast_to(
                    (128, 8, 64))
                P.do("gpsimd", lambda g, dcol=dcol: g.tensor_mul(
                    Srun_sb.rearrange("p (q j) -> p q j", j=64),
                    Srun_sb.rearrange("p (q j) -> p q j", j=64), dcol), None)
                P.do("gpsimd", lambda g, cl=cl: g.tensor_add(
                    Srun_sb, Srun_sb, M_sb[:, cl % 2, :]), None)
                P.do("gpsimd", lambda g, cl=cl, s=s: g.tensor_mul(
                    Drun_sb, Drun_sb, D_all[:, 16 * s + cl, :]), "gp")
                sum_done = P.rec("gp")
                if stop_after == "m1" and cl == 0:
                    break
                if stop_after == "m4" and cl == 3:
                    break

            P.barrier()
            if stop_after in ("m1", "m4", "m16"):
                _close(P, nc, ctx, [])
                return nc

        if stop_after == "seg":
            tap("Mrun", Srun_sb)
            tap("Drun", Drun_sb)
            tap("Dall", D_all.rearrange("p c m -> p (c m)"))
            tap("cend", cend_all.rearrange("p c m -> p (c m)"))
            tap("mra", mr_all.rearrange("p c m -> p (c m)"))
            tap("mka", mk_all.rearrange("p c m -> p (c m)"))
            _close(P, nc, ctx, [])
            return nc
        tap("Mrun", Srun_sb)
        tap("Drun", Drun_sb)
        tap("Dall", D_all.rearrange("p c m -> p (c m)"))
        tap("cend", cend_all.rearrange("p c m -> p (c m)"))
        tap("mra", mr_all.rearrange("p c m -> p (c m)"))
        tap("mka", mk_all.rearrange("p c m -> p (c m)"))

        P.phase("gather")
        P.dma("sync", cc_in[:, 0:512], Srun_sb, "st")
        cc2 = P.dma("sync", cc_in[:, 512:520], Drun_sb, "st")
        P.waits("gpsimd", [cc2])
        P.do("gpsimd", lambda g: g.collective_compute(
            "AllGather", ALU.bypass, replica_groups=[list(range(NCORE))],
            ins=[cc_in], outs=[cc_out]), "cc")
        ar.reset(ATW0)
        GTH0 = ar.mark()
        gath_sb = ar.f32(8, 520)
        deff_sb = ar.f32(8)
        tS_sb = ar.f32(512)
        P.waits("sync", [P.rec("cc")])
        P.dma("sync", gath_sb, cc_out.rearrange("(j p) w -> p j w", p=128),
              "ldc")
        tap("Spre", Spp[:, 0])
        P.wait("vector", "ldc")
        for j in range(NCORE - 1):
            P.do("vector", lambda v, j=j: v.tensor_scalar(
                deff_sb, gath_sb[:, j, 512:520], pm_sb[:, j:j + 1], None,
                ALU.mult), None)
            P.do("vector", lambda v, j=j: v.tensor_scalar(
                deff_sb, deff_sb, pm1_sb[:, j:j + 1], None, ALU.add), None)
            P.do("vector", lambda v: v.tensor_mul(
                tS_sb.rearrange("p (q j2) -> p q j2", j2=64),
                Spp[:, 0].rearrange("p (q j2) -> p q j2", j2=64),
                deff_sb.unsqueeze(2).broadcast_to((128, 8, 64))), None)
            P.do("vector", lambda v, j=j: v.scalar_tensor_tensor(
                Spp[:, 0].rearrange("p (q j2) -> p q j2", j2=64),
                gath_sb[:, j, 0:512].rearrange("p (q j2) -> p q j2", j2=64),
                pm_sb[:, j:j + 1],
                tS_sb.rearrange("p (q j2) -> p q j2", j2=64),
                ALU.mult, ALU.add), "ve")
        SIN = P.rec("ve")
        P.barrier()
        tap("deff", deff_sb)
        tap("gath", gath_sb.rearrange("p j w -> p (j w)"))
        tap("Sin", Spp[:, 0])

        if stop_after == "pass1":
            _close(P, nc, ctx, block_holder := [])
            return nc

        ar.reset(GTH0)
        oWt_sb = ar.bf16(8, D)
        P.dma("gpsimd", oWt_sb.rearrange("p m o -> p (m o)"), oWtt, "gc")
        rt2 = [ar.bf16(8, 4, PADC) for _ in range(2)]
        kt2 = [ar.bf16(8, 4, PADC) for _ in range(2)]
        ct2 = [ar.f32(8, 4, PADC) for _ in range(2)]
        vt2 = [ar.bf16(D) for _ in range(2)]
        gt2 = [ar.bf16(D) for _ in range(2)]
        xt2 = [ar.f32(D) for _ in range(2)]
        m42 = [ar.f32(4, 512) for _ in range(2)]
        ab2 = [ar.bf16(2048) for _ in range(2)]
        wtil_s = ar.f32(8, 4, 32)
        wstp_s = ar.f32(8, 4, PADC)
        wpro_s = ar.f32(8, 4, 32)
        er2 = [ar.bf16(8, 4, 32) for _ in range(2)]
        rst2 = [ar.bf16(8, 4, 32) for _ in range(2)]
        rA2 = [ar.bf16(8, 4, 32) for _ in range(2)]
        kA2 = [ar.bf16(8, 4, 32) for _ in range(2)]
        P82 = [ar.bf16(8, 8, 128) for _ in range(2)]
        Asb2 = [ar.bf16(512) for _ in range(2)]
        ad82 = [ar.bf16(8, 128) for _ in range(2)]
        strip = ar.bf16(8, 16)
        ysq = ar.f32(D)
        rl_sb = ar.f32(D)
        yt_sb = ar.f32(D)
        t2_sb = ar.f32(D)
        o_tm = ar.bf16(D)
        sg_sb = ar.bf16(D)
        o_fm = ar.bf16(8, 128)
        s1_sb = ar.f32(D)
        x1_sb = ar.f32(D)
        st16 = ar.f32(16, 4)
        Sbf = ar.bf16(2, 512)
        eps_col = ar.f32(1)
        P.do("vector", lambda v: v.memset(eps_col, EPS), "ve")
        PK2 = ar.mark()

        ltsem = {0: "ltA", 1: "ltB"}
        scsem = {0: "scA", 1: "scB"}

        def prefetch_tile(a):
            pi = a % 2
            sname = ltsem[pi]
            sgl, al = a // 4, a % 4
            P.dma("sync", rt2[pi].rearrange("p m c w -> p (m c w)"),
                  r_d[sgl, al], sname)
            P.dma("sync", kt2[pi].rearrange("p m c w -> p (m c w)"),
                  k_d[sgl, al], sname)
            P.dma("sync", ct2[pi].rearrange("p m c w -> p (m c w)"),
                  c_d[sgl, al], sname)
            P.dma("sync", vt2[pi], v_d[a], sname)
            P.dma("sync", gt2[pi], g_d[a], sname)
            P.dma("sync", xt2[pi], x_hbm[128 * a:128 * (a + 1), :], sname)
            P.dma("sync", m42[pi],
                  M_d[4 * a:4 * a + 4].rearrange("c p w -> p c w"), sname)

        P.waits("vector", [SIN])
        P.do("vector", lambda v: v.tensor_copy(Sbf[:, 0], Spp[:, 0]), "ve")
        SINB = P.rec("ve")
        spp_ready0 = SINB
        prefetch_tile(0)
        bank0_free = ("ve", 0)
        bank45_free = ("ve", 0)
        oW_free = ("ve", 0)
        strip_ev = ("ve", 0)
        ofm_ev = {}
        spp_ready = {0: spp_ready0, 1: ("ve", 0)}
        Z = ("ve", 0)
        OTM_prev = Z
        R1_prev = X1_prev = Z
        x1st_prev = 0
        tile_rel = {0: [], 1: []}
        ymm_hist = {0: ("pe", 0), 1: ("pe", 0)}
        ak2_hist = {0: ("sc", 0), 1: ("sc", 0)}
        pband_hist = {0: ("gp", 0), 1: ("gp", 0)}
        red_hist = {0: ("pe", 0), 1: ("pe", 0)}
        adb_hist = {0: ("ve", 0), 1: ("ve", 0)}
        trd_hist = {0: ("pe", 0), 1: ("pe", 0)}
        OFM_prev = ("ve", 0)
        def emit_tile(a):
            nonlocal bank0_free, bank45_free, oW_free, strip_ev
            nonlocal OTM_prev, OFM_prev
            nonlocal R1_prev, X1_prev, x1st_prev
            pi = a % 2
            wtil, wstp, wpro = wtil_s, wstp_s, wpro_s
            er_b, rst, rA, kA = er2[pi], rst2[pi], rA2[pi], kA2[pi]
            P8, A_sb, adbf8 = P82[pi], Asb2[pi], ad82[pi]
            RED_prev = red_hist[pi]
            if a + 1 < NTILE:
                P.waits("sync", tile_rel[(a + 1) % 2])
                prefetch_tile(a + 1)
            rt, kt, ct = rt2[pi], kt2[pi], ct2[pi]
            vt, gt, xt, m4 = vt2[pi], gt2[pi], xt2[pi], m42[pi]
            if stop_after == "p2z" and a == 0:
                return True
            P.wait("vector", ltsem[pi])
            P.wait("scalar", ltsem[pi])
            P.wait("tensor", ltsem[pi])
            ctv = ct[:, :, :, PAD0:PADC]
            ctp = ct[:, :, :, PAD0 - 1:PADC - 1]
            P.waits("vector", [ak2_hist[pi], ak2_hist[1 - pi],
                               ymm_hist[pi], pband_hist[pi]])
            P.do("vector", lambda v, wtil=wtil, ctv=ctv, ctp=ctp:
                 v.tensor_sub(wtil, ctv, ctp), "ve")
            P.do("vector", lambda v: v.memset(
                wstp.rearrange("p m c w -> p (m c w)"), 0.0), "ve")
            WV = P.rec("ve")
            P.waits("scalar", [WV, pband_hist[pi], pband_hist[1 - pi]])
            P.do("scalar", lambda e, wstp=wstp, wtil=wtil: e.activation(
                wstp[:, :, :, PAD0:PADC], wtil, AF.Exp, scale=-1.0), "sc")
            WSTP = P.rec("sc")
            P.do("scalar", lambda e: e.activation(
                er_b, ctp, AF.Exp, scale=-1.0), "sc")
            ER = P.rec("sc")
            P.waits("vector", [ER])
            P.do("vector", lambda v: v.tensor_mul(
                rst, rt[:, :, :, PAD0:PADC], er_b), "ve")
            mrv = mr_all[:, 4 * a:4 * a + 4, :].transpose([0, 2, 1]).unsqueeze(
                3).broadcast_to((128, 8, 4, 32))
            mkv = mk_all[:, 4 * a:4 * a + 4, :].transpose([0, 2, 1]).unsqueeze(
                3).broadcast_to((128, 8, 4, 32))
            P.do("vector", lambda v, mrv=mrv: v.tensor_sub(wtil, mrv, ctp), "ve")
            AR1 = P.rec("ve")
            P.waits("scalar", [AR1])
            P.do("scalar", lambda e: e.activation(er_b, wtil, AF.Exp), "sc")
            AR2 = P.rec("sc")
            P.waits("vector", [AR2])
            P.do("vector", lambda v: v.tensor_mul(
                rA, rt[:, :, :, PAD0:PADC], er_b), "ve")
            P.do("vector", lambda v, mkv=mkv: v.tensor_sub(wtil, ctv, mkv), "ve")
            AK1 = P.rec("ve")
            P.waits("scalar", [AK1])
            P.do("scalar", lambda e: e.activation(er_b, wtil, AF.Exp), "sc")
            AK2 = P.rec("sc")
            P.waits("vector", [AK2])
            P.do("vector", lambda v: v.tensor_mul(
                kA, kt[:, :, :, PAD0:PADC], er_b), "ve")
            PREPS = P.rec("ve")
            if stop_after == "p2pre" and a == 0:
                return True

            P.waits("tensor", [PREPS, bank0_free if a > 0 else ("ve", 0)])
            for par in (0, 1):
                for cc in range(4):
                    for h in range(par, 16, 2):
                        P.pe_mm(lambda t, cc=cc, h=h: t.matmul(
                            pbank(0)[32 * cc:32 * cc + 32, 32 * h:32 * h + 32],
                            kA[64 * (h % 2):64 * (h % 2) + 64, h // 2, cc, :],
                            rA[64 * (h % 2):64 * (h % 2) + 64, h // 2, cc, :],
                            start=True, stop=True,
                            tile_position=(64 * (h % 2), 32 * cc)),
                            key=("A", par), sem="pe")
            AMM = P.rec("pe")
            if stop_after == "p2amm" and a == 0:
                return True
            P.waits("vector", [AMM, ymm_hist[pi]])
            P.do("vector", lambda v, A_sb=A_sb: v.tensor_mul(
                A_sb, pbank(0), mscale_sb), "ve")
            bank0_free = P.rec("ve")
            if stop_after == "p2a" and a == 0:
                tap("Asb0", A_sb)
                return True

            P.do("gpsimd", lambda g: g.memset(
                wpro.rearrange("p m c w -> p (m c w)"), 1.0), "gp")
            scat_cnt0 = P.val(scsem[pi])
            P.waits("gpsimd", [RED_prev])
            for dl in range(0, BAND + 1):
                pv = P8[:, dl, :, :].rearrange("p m (c w) -> p m c w", w=32)
                if dl >= 2:
                    if dl == 2:
                        P.waits("gpsimd", [WSTP])
                    P.do("gpsimd", lambda g, dl=dl: g.tensor_mul(
                        wpro, wpro,
                        wstp[:, :, :, PAD0 - (dl - 1):PADC - (dl - 1)]), "gp")
                if dl == 0:
                    ub = uc_sb.unsqueeze(2).unsqueeze(3).broadcast_to(
                        (128, 8, 4, 32))
                    P.do("gpsimd", lambda g, ub=ub, pv=pv: g.tensor_mul(
                        pv, rt[:, :, :, PAD0:PADC], ub), "gp")
                    P.do("gpsimd", lambda g, pv=pv: g.tensor_mul(
                        pv, pv, kt[:, :, :, PAD0:PADC]), "gp")
                elif dl == 1:
                    P.do("gpsimd", lambda g, pv=pv: g.tensor_mul(
                        pv, rt[:, :, :, PAD0:PADC],
                        kt[:, :, :, PAD0 - 1:PADC - 1]), "gp")
                else:
                    P.do("gpsimd", lambda g, pv=pv: g.tensor_mul(
                        pv, rt[:, :, :, PAD0:PADC], wpro), "gp")
                    P.do("gpsimd", lambda g, dl=dl, pv=pv: g.tensor_mul(
                        pv, pv, kt[:, :, :, PAD0 - dl:PADC - dl]), "gp")
            PBAND = P.rec("gp")
            if a == 0:
                P.waits("sync", [PBAND])
                tap("rt0", rt.rearrange("p m c w -> p (m c w)"))
                tap("kt0", kt.rearrange("p m c w -> p (m c w)"))
                tap("P0", P8[:, 0, :, :].rearrange("p m t -> p (m t)"))
                tap("uc0", uc_sb)
                if stop_after == "band0":
                    return True
            P.waits("tensor", [PBAND, adb_hist[pi], OFM_prev])
            for dl in range(8):
                for m in range(8):
                    P.do("tensor", lambda t, m=m, dl=dl: t.matmul(
                        pbank(1 + dl // 4)[0:16,
                                           128 * (dl % 4):128 * (dl % 4) + 128],
                        ind16_sb[:, m, :], P8[:, dl, m, :],
                        start=(m == 0), stop=(m == 7)), "pe")
            RED_prev = P.rec("pe")
            P.waits("vector", [RED_prev, trd_hist[pi]])
            for dl in range(8):
                P.do("vector", lambda v, dl=dl: v.tensor_copy(
                    adbf8[0:16, dl, :],
                    pbank(1 + dl // 4)[0:16,
                                       128 * (dl % 4):128 * (dl % 4) + 128]),
                    "ve")
            ADB = P.rec("ve")
            adb_hist[pi] = ADB
            P.waits("tensor", [ADB, strip_ev])
            for dl in range(8):
                P.do("tensor", lambda t, dl=dl: t.transpose(
                    pall[:, 1536 + 8 * dl:1544 + 8 * dl].bitcast(BF16),
                    adbf8[0:16, dl, :], ident_sb[0:16, 0:16]), "pe")
            TRDall = P.rec("pe")
            trd_hist[pi] = TRDall
            P.waits("vector", [TRDall])
            P.wait("vector", "scA")
            P.wait("vector", "scB")
            for dl in range(8):
                P.do("vector", lambda v, dl=dl: v.tensor_copy(
                    strip[:, dl, :],
                    pall[:, 1536 + 8 * dl:1544 + 8 * dl].bitcast(BF16)), "ve")
            strip_ev = P.rec("ve")
            P.waits("sync", [strip_ev])
            for dl in range(0, BAND + 1):
                if dl == 0:
                    out_ap = bass.AP(ab_scr.tensor, pi * (128 * 128 * 16),
                                     [[2064, 128], [1, 16]])
                    P.dma("sync", out_ap, strip[:, 0, :], scsem[pi])
                else:
                    out_ap = bass.AP(ab_scr.tensor,
                                     pi * (128 * 128 * 16) + dl * 16,
                                     [[2064, 128 - dl], [1, 16]])
                    P.dma("sync", out_ap, strip[dl:128, dl, :], scsem[pi])
            P.waits("sync", [ymm_hist[pi]])
            P.wait("sync", scsem[pi])
            P.dma("sync", ab2[pi],
                  ab_scr[pi].rearrange("(s x) -> s x", s=128), ltsem[pi])
            if stop_after == "p2b" and a == 0:
                P.wait("sync", ltsem[pi])
                tap("aband0", ab2[pi])
                return True

            P.wait("tensor", ltsem[pi])
            P.waits("tensor", [bank45_free if a > 0 else ("ve", 0)])
            abv = ab2[pi].rearrange("p (t h) -> p h t", h=16)
            for cc in range(4):
                ci = 4 * a + cc
                P.waits("tensor", [spp_ready[ci % 2]])
                for h in range(16):
                    P.pe_mm(lambda t, cc=cc, h=h, ci=ci: t.matmul(
                        pbank(4, 1024)[32 * cc:32 * cc + 32, 64 * h:64 * h + 64],
                        rst[64 * (h % 2):64 * (h % 2) + 64, h // 2, cc, :],
                        Sbf[64 * (h % 2):64 * (h % 2) + 64, ci % 2,
                            64 * (h // 2):64 * (h // 2) + 64],
                        start=True, stop=False,
                        tile_position=(64 * (h % 2), 32 * cc)),
                        key=("yst", h % 2), sem="pe")
                    P.pe_mm(lambda t, cc=cc, h=h: t.matmul(
                        pbank(4, 1024)[32 * cc:32 * cc + 32, 64 * h:64 * h + 64],
                        A_sb[32 * cc:32 * cc + 32, 32 * h:32 * h + 32],
                        vt[32 * cc:32 * cc + 32, 64 * h:64 * h + 64],
                        start=False, stop=False,
                        tile_position=(32 * cc, 32 * cc)),
                        key=("y1", cc), sem="pe")
                    P.pe_mm(lambda t, cc=cc, h=h: t.matmul(
                        pbank(4, 1024)[32 * cc:32 * cc + 32, 64 * h:64 * h + 64],
                        abv[:, h, 32 * cc:32 * cc + 32],
                        vt[:, 64 * h:64 * h + 64],
                        start=False, stop=True,
                        tile_position=(0, 32 * cc)),
                        key=("y2", 0), sem="pe")
                YST = P.rec("pe")
                P.waits("vector", [YST])
                dcolb = D_all[:, ci, :].unsqueeze(2).broadcast_to((128, 8, 64))
                P.do("vector", lambda v, ci=ci, dcolb=dcolb: v.tensor_mul(
                    Spp[:, (ci + 1) % 2].rearrange("p (q j) -> p q j", j=64),
                    Spp[:, ci % 2].rearrange("p (q j) -> p q j", j=64),
                    dcolb), "ve")
                P.do("vector", lambda v, ci=ci, cc=cc: v.tensor_add(
                    Spp[:, (ci + 1) % 2], Spp[:, (ci + 1) % 2],
                    m4[:, cc, :]), "ve")
                P.do("vector", lambda v, ci=ci: v.tensor_copy(
                    Sbf[:, (ci + 1) % 2], Spp[:, (ci + 1) % 2]), "ve")
                spp_ready[(ci + 1) % 2] = P.rec("ve")
            YMM = P.rec("pe")

            P.waits("vector", [YMM, OTM_prev])
            yv = pbank(4, 1024).rearrange("p (h j) -> p h j", j=64)
            P.do("vector", lambda v: v.reduce_sum(
                st16[:, :, 0], yv, axis=mybir.AxisListType.X), "ve")
            MU0 = P.rec("ve")
            P.waits("scalar", [YMM, X1_prev])
            P.do("scalar", lambda e: e.activation(
                ysq.rearrange("p (h j) -> p h j", j=64), yv, AF.Square), "sc")
            SQ = P.rec("sc")
            P.waits("vector", [SQ])
            P.do("vector", lambda v: v.reduce_sum(
                st16[:, :, 1], ysq.rearrange("p (h j) -> p h j", j=64),
                axis=mybir.AxisListType.X), "ve")
            P.do("vector", lambda v: v.tensor_scalar(
                st16[:, :, 0], st16[:, :, 0], 1.0 / 64, 0.0, ALU.mult,
                ALU.add), "ve")
            P.do("vector", lambda v: v.tensor_scalar(
                st16[:, :, 1], st16[:, :, 1], 1.0 / 64, 0.0, ALU.mult,
                ALU.add), "ve")
            P.do("vector", lambda v: v.tensor_mul(
                st16[:, :, 2], st16[:, :, 0], st16[:, :, 0]), "ve")
            P.do("vector", lambda v: v.tensor_sub(
                st16[:, :, 1], st16[:, :, 1], st16[:, :, 2]), "ve")
            VAR = P.rec("ve")
            P.waits("scalar", [VAR])
            P.do("scalar", lambda e: e.activation(
                st16[:, :, 3], st16[:, :, 1], AF.Ln, bias=eps_col), "sc")
            P.do("scalar", lambda e: e.activation(
                st16[:, :, 3], st16[:, :, 3], AF.Exp, scale=-0.5), "sc")
            RSTD = P.rec("sc")
            P.waits("vector", [RSTD])
            P.do("vector", lambda v: v.tensor_mul(
                rl_sb.rearrange("p (h j) -> p h j", j=64),
                st16[:, :, 3].unsqueeze(2).broadcast_to((128, 16, 64)),
                lnw_sb.rearrange("p (h j) -> p h j", j=64)), "ve")
            P.do("vector", lambda v: v.tensor_sub(
                yt_sb.rearrange("p (h j) -> p h j", j=64), yv,
                st16[:, :, 0].unsqueeze(2).broadcast_to((128, 16, 64))), "ve")
            YT = P.rec("ve")
            bank45_free = YT
            P.waits("scalar", [OTM_prev])
            P.do("scalar", lambda e, gt=gt: e.activation(sg_sb, gt, AF.Silu), "sc")
            SG = P.rec("sc")
            P.waits("gpsimd", [YT, SG])
            P.do("gpsimd", lambda g: g.tensor_mul(t2_sb, yt_sb, rl_sb), "gp")
            P.do("gpsimd", lambda g: g.tensor_add(t2_sb, t2_sb, lnb_sb), "gp")
            P.do("gpsimd", lambda g: g.tensor_mul(o_tm, t2_sb, sg_sb), "gp")
            OTM = P.rec("gp")
            if stop_after == "p2y" and a == 0:
                P.waits("sync", [OTM])
                tap("ytile0", yt_sb)
                return True

            P.waits("tensor", [OTM, ADB, strip_ev])
            for m in range(8):
                trb = 2 + (m % 2)
                if m >= 2:
                    P.waits("tensor", [ofm_ev[m - 2]])
                P.do("tensor", lambda t, m=m, trb=trb: t.transpose(
                    pbank_bf(trb, 128)[:, 0:128],
                    o_tm[:, 128 * m:128 * (m + 1)], ident_sb), "pe")
                TRO = P.rec("pe")
                P.waits("vector", [TRO])
                P.do("vector", lambda v, m=m, trb=trb: v.tensor_copy(
                    o_fm[:, m, :], pbank_bf(trb, 128)[:, 0:128]), "ve")
                ofm_ev[m] = P.rec("ve") if True else None
            OFM = P.rec("ve")
            OFM_prev = OFM

            P.waits("tensor", [OFM, oW_free if a > 0 else ("ve", 0)])
            P.wait("tensor", "gc")
            for n in range(2):
                for kk in range(8):
                    P.do("tensor", lambda t, n=n, kk=kk: t.matmul(
                        pbank(6 + n), o_fm[:, kk, :],
                        oWt_sb[:, kk, 512 * n:512 * (n + 1)],
                        start=(kk == 0), stop=(kk == 7)), "pe")
            OWM = P.rec("pe")
            P.waits("vector", [OWM, R1_prev])
            for n in range(2):
                P.do("vector", lambda v, n=n: v.tensor_add(
                    s1_sb[:, 512 * n:512 * (n + 1)], pbank(6 + n),
                    xt[:, 512 * n:512 * (n + 1)]), "ve")
            oW_free = P.rec("ve")
            P.do("vector", lambda v: v.reduce_sum(
                st16[:, 0, 0:1], s1_sb.rearrange("p (o d) -> p o d", o=1),
                axis=mybir.AxisListType.X), "ve")
            S1M = P.rec("ve")
            P.waits("scalar", [S1M])
            P.do("scalar", lambda e: e.activation(ysq, s1_sb, AF.Square), "sc")
            SQ1 = P.rec("sc")
            P.waits("vector", [SQ1])
            P.do("vector", lambda v: v.reduce_sum(
                st16[:, 1, 0:1], ysq.rearrange("p (o d) -> p o d", o=1),
                axis=mybir.AxisListType.X), "ve")
            P.do("vector", lambda v: v.tensor_scalar(
                st16[:, 0, 0:1], st16[:, 0, 0:1], 1.0 / D, 0.0, ALU.mult,
                ALU.add), "ve")
            P.do("vector", lambda v: v.tensor_scalar(
                st16[:, 1, 0:1], st16[:, 1, 0:1], 1.0 / D, 0.0, ALU.mult,
                ALU.add), "ve")
            P.do("vector", lambda v: v.tensor_mul(
                st16[:, 2, 0:1], st16[:, 0, 0:1], st16[:, 0, 0:1]), "ve")
            P.do("vector", lambda v: v.tensor_sub(
                st16[:, 1, 0:1], st16[:, 1, 0:1], st16[:, 2, 0:1]), "ve")
            V1 = P.rec("ve")
            P.waits("scalar", [V1])
            P.do("scalar", lambda e: e.activation(
                st16[:, 3, 0:1], st16[:, 1, 0:1], AF.Ln, bias=eps_col), "sc")
            P.do("scalar", lambda e: e.activation(
                st16[:, 3, 0:1], st16[:, 3, 0:1], AF.Exp, scale=-0.5), "sc")
            R1 = P.rec("sc")
            P.waits("vector", [R1])
            if x1st_prev:
                P.wait("vector", "st", x1st_prev)
            P.do("vector", lambda v: v.tensor_scalar(
                x1_sb, s1_sb, st16[:, 0, 0:1], st16[:, 3, 0:1],
                ALU.subtract, ALU.mult), "ve")
            X1 = P.rec("ve")
            P.waits("sync", [X1])
            P.dma("sync", x1_d[a], x1_sb, "st")
            x1st_prev = P.val("st")
            SUPD = P.rec("ve")
            tile_rel[pi] = [PBAND, PREPS, YMM, SG, oW_free, SUPD]
            ymm_hist[pi] = YMM
            ak2_hist[pi] = AK2
            pband_hist[pi] = PBAND
            red_hist[pi] = RED_prev
            OTM_prev, R1_prev, X1_prev = OTM, R1, X1
            if a == 0:
                tap("x1t0", x1_sb)
                tap("ytile0", yt_sb)
                tap("Asb0", A_sb)
                tap("aband0", ab2[pi])

        for a in range(NTILE):
            P.phase(f"p2t{a}")
            if emit_tile(a):
                _close(P, nc, ctx, [])
                return nc
        P.barrier()

        if _emit_ffn(P, nc, ar, pbank, pbank_bf, dict(locals())):
            _close(P, nc, ctx, [])
            return nc

        P.wait("sync", "outst")
        _close(P, nc, ctx, [])

    return nc


def _close(P, nc, ctx, _):
    def runner(eng):
        marks = {}
        for name, idxs in P.phases:
            marks[idxs[eng]] = name

        def run(e):
            cur = None
            sid = None
            for i, f in enumerate(P.ops[eng]):
                if i in marks:
                    if cur is not None:
                        nc.leave_named_scope(cur, sid, False)
                    cur = marks[i]
                    sid, _ = nc.enter_named_scope(cur, False)
                f(e)
            if cur is not None:
                nc.leave_named_scope(cur, sid, False)
        return run

    block = ctx.enter_context(nc.Block())
    block.sync(runner("sync"))
    block.gpsimd(runner("gpsimd"))
    block.vector(runner("vector"))
    block.scalar(runner("scalar"))
    block.tensor(runner("tensor"))


def _emit_pass2(P, nc, ar, pbank, pbank_bf, L):
    pass


def _emit_ffn(P, nc, ar, pbank, pbank_bf, L):
    x1_d, ha_d, hb_d, o3_d = L["x1_d"], L["ha_d"], L["hb_d"], L["o3_d"]
    w12t, h1tt, h2tt, w3tt = L["w12t"], L["h1tt"], L["h2tt"], L["w3tt"]
    out_hbm, ident_in = L["out_hbm"], L["ident_in"]
    scols_in, w3s_in = L["scols_in"], L["w3s_in"]
    tap, stop_after, ltsem = L["tap"], L["stop_after"], L["ltsem"]
    fwsem = {0: "fwA", 1: "fwB"}

    ar.reset(0)
    idf = ar.bf16(128)
    scol2 = ar.f32(4, 32)
    w3s2 = ar.f32(D)
    eps5 = ar.f32(1)
    FFNB = ar.mark()
    h_sb = ar.bf16(32, TC)
    FFN0 = ar.mark()
    P.dma("sync", idf, ident_in, "ldc")
    P.dma("sync", scol2, scols_in, "ldc")
    P.dma("sync", w3s2, w3s_in, "ldc")
    P.do("vector", lambda v: v.memset(eps5, EPS), "ve")
    P.wait("tensor", "ldc")
    P.wait("scalar", "ldc")
    P.wait("vector", "ldc")

    P.phase("ffnx")
    x1_fm = ar.bf16(8, TC)
    wf = [ar.bf16(16, 128), ar.bf16(16, 128)]
    sg1 = [ar.f32(512), ar.f32(512)]
    x1bb = [ar.bf16(D), ar.bf16(D)]
    TRlast = {0: ("pe", 0), 1: ("pe", 0)}
    ev_m = {}
    for a in range(NTILE):
        pi = a % 2
        P.waits("gpsimd", [TRlast[pi]])
        P.dma("gpsimd", x1bb[pi], x1_d[a], "gx")
        P.wait("tensor", "gx")
        for m in range(8):
            trb = 2 + (m % 2)
            if a * 8 + m >= 2:
                P.waits("tensor", [ev_m[a * 8 + m - 2]])
            P.do("tensor", lambda t, m=m, trb=trb, pi=pi: t.transpose(
                pbank_bf(trb, 128)[:, 0:128],
                x1bb[pi][:, 128 * m:128 * m + 128], idf), "pe")
            TR = P.rec("pe")
            P.waits("vector", [TR])
            P.do("vector", lambda v, m=m, trb=trb, a=a: v.tensor_copy(
                x1_fm[:, m, 128 * a:128 * a + 128],
                pbank_bf(trb, 128)[:, 0:128]), "ve")
            ev_m[a * 8 + m] = P.rec("ve")
        TRlast[pi] = P.rec("pe")
    X1FM = P.rec("ve")
    if stop_after == "ffnx":
        P.waits("sync", [X1FM])
        tap("x1fm0", x1_fm[:, 0, :])
        tap("x1fm7", x1_fm[:, 7, :])
        return True

    GMlast = {0: ("pe", 0), 1: ("pe", 0)}

    def w12_dma(oc, pi):
        P.dma("gpsimd", wf[pi].rearrange("p j o -> p (j o)"),
              w12t[oc], fwsem[pi])

    P.phase("ffn1")
    w12_dma(0, 0)
    ev_g = {0: ("ve", 0), 1: ("ve", 0)}
    sg_free = {0: ("ve", 0), 1: ("ve", 0)}
    for oc in range(32):
        pi = oc % 2
        if oc + 1 < 32:
            P.waits("gpsimd", [GMlast[(oc + 1) % 2]])
            w12_dma(oc + 1, (oc + 1) % 2)
        P.wait("tensor", fwsem[pi])
        if oc == 0:
            P.waits("tensor", [X1FM])
        for tp in range(2):
            g = (2 * oc + tp) % 2
            b0 = 4 * g
            P.waits("tensor", [ev_g[g]])
            for w in range(2):
                for tt in range(2):
                    tix = 2 * tp + tt
                    for kk in range(8):
                        P.pe_mm(lambda t, w=w, tt=tt, tix=tix, kk=kk, pi=pi,
                                b0=b0: t.matmul(
                            pbank(b0 + 2 * w + tt), wf[pi][:, 8 * w + kk, :],
                            x1_fm[:, kk, 512 * tix:512 * tix + 512],
                            start=(kk == 0), stop=(kk == 7)),
                            key=("f",), sem="pe")
            GM = P.rec("pe")
            GMlast[pi] = GM
            for tt in range(2):
                tix = 2 * tp + tt
                P.waits("scalar", [GM, sg_free[tt]])
                P.do("scalar", lambda e, b0=b0, tt=tt, oc=oc: e.activation(
                    sg1[tt], pbank(b0 + tt), AF.Silu,
                    scale=scol2[:, 0, oc:oc + 1]), "sc")
                SGr = P.rec("sc")
                P.waits("vector", [SGr])
                P.do("vector", lambda v, b0=b0, tt=tt, oc=oc, tix=tix:
                     v.scalar_tensor_tensor(
                         h_sb[:, oc, 512 * tix:512 * tix + 512],
                         pbank(b0 + 2 + tt), scol2[:, 1, oc:oc + 1],
                         sg1[tt], ALU.mult, ALU.mult), "ve")
                sg_free[tt] = P.rec("ve")
            ev_g[g] = P.rec("ve")
    if stop_after == "ffn1":
        tap("hs0", h_sb[:, 0, :])
        tap("hs31", h_sb[:, 31, :])
        return True

    P.barrier()
    ar.reset(FFN0)
    hw = [ar.bf16(32, 128), ar.bf16(32, 128)]
    hout = [ar.bf16(TC), ar.bf16(TC)]

    def emit_mid(wt_d, sidx, dst_d):
        MMl = {0: ("pe", 0), 1: ("pe", 0)}

        def hw_dma(oc, pi):
            P.dma("gpsimd", hw[pi].rearrange("p j o -> p (j o)"),
                  wt_d[oc], fwsem[pi])

        hw_dma(0, 0)
        evl = {0: ("sc", 0), 1: ("sc", 0)}
        stc = {0: 0, 1: 0}
        for oc in range(32):
            pi = oc % 2
            if oc + 1 < 32:
                P.waits("gpsimd", [MMl[(oc + 1) % 2]])
                hw_dma(oc + 1, (oc + 1) % 2)
            P.wait("tensor", fwsem[pi])
            P.waits("tensor", [evl[pi]])
            for kk in range(32):
                for t in range(4):
                    P.pe_mm(lambda t_, t=t, kk=kk, pi=pi: t_.matmul(
                        pbank(4 * pi + t), hw[pi][:, kk, :],
                        h_sb[:, kk, 512 * t:512 * t + 512],
                        start=(kk == 0), stop=(kk == 31)),
                        key=("f",), sem="pe")
            MM = P.rec("pe")
            MMl[pi] = MM
            P.waits("scalar", [MM])
            if stc[pi]:
                P.wait("scalar", "st", stc[pi])
            for t in range(4):
                P.do("scalar", lambda e, t=t, pi=pi, oc=oc, sidx=sidx:
                     e.activation(hout[pi][:, 512 * t:512 * t + 512],
                                  pbank(4 * pi + t), AF.Gelu,
                                  scale=scol2[:, sidx, oc:oc + 1]), "sc")
            EV = P.rec("sc")
            evl[pi] = EV
            P.waits("sync", [EV])
            P.dma("sync", dst_d[oc], hout[pi], "st")
            stc[pi] = P.val("st")

    def emit_reload(src_d):
        P.barrier()
        for c in range(32):
            if c % 2 == 0:
                P.dma("sync", h_sb[:, c, :], src_d[c], "ldc")
            else:
                P.dma("gpsimd", h_sb[:, c, :], src_d[c], "gc")
        P.wait("tensor", "ldc")
        P.wait("tensor", "gc")

    P.phase("ffn2")
    emit_mid(h1tt, 2, ha_d)
    emit_reload(ha_d)
    if stop_after == "ffn2":
        tap("ha0", h_sb[:, 0, :])
        tap("ha31", h_sb[:, 31, :])
        return True
    P.phase("ffn3")
    emit_mid(h2tt, 3, hb_d)
    emit_reload(hb_d)
    if stop_after == "ffn3":
        tap("hb0", h_sb[:, 0, :])
        tap("hb31", h_sb[:, 31, :])
        return True

    P.phase("ffn4")
    ar.reset(FFN0)
    w3q = [ar.bf16(32, 256), ar.bf16(32, 256)]
    o3e = [ar.f32(256), ar.f32(256)]
    MMc = {0: ("pe", 0), 1: ("pe", 0)}

    def w3_dma(c4, pi):
        P.dma("gpsimd", w3q[pi].rearrange("p j o -> p (j o)"),
              w3tt[c4], fwsem[pi])

    w3_dma(0, 0)
    evb = {0: ("ve", 0), 1: ("ve", 0)}
    sto = {0: 0, 1: 0}
    for c4 in range(4):
        pi = c4 % 2
        if c4 + 1 < 4:
            P.waits("gpsimd", [MMc[(c4 + 1) % 2]])
            w3_dma(c4 + 1, (c4 + 1) % 2)
        P.wait("tensor", fwsem[pi])
        for a in range(NTILE):
            pb = a % 2
            P.waits("tensor", [evb[pb]])
            for kk in range(32):
                P.pe_mm(lambda t, kk=kk, a=a, pi=pi, pb=pb: t.matmul(
                    pbank(pb, 256), h_sb[:, kk, 128 * a:128 * a + 128],
                    w3q[pi][:, kk, :],
                    start=(kk == 0), stop=(kk == 31)),
                    key=("w3",), sem="pe")
            MM = P.rec("pe")
            MMc[pi] = MM
            P.waits("vector", [MM])
            if sto[pb]:
                P.wait("vector", "st", sto[pb])
            P.do("vector", lambda v, pb=pb, c4=c4: v.tensor_mul(
                o3e[pb], pbank(pb, 256),
                w3s2[:, 256 * c4:256 * c4 + 256]), "ve")
            EVV = P.rec("ve")
            evb[pb] = EVV
            P.waits("sync", [EVV])
            P.dma("sync", o3_d[a][:, 256 * c4:256 * c4 + 256], o3e[pb], "st")
            sto[pb] = P.val("st")

    P.phase("ffn5")
    P.barrier()
    ar.reset(FFNB)
    x1r = ar.f32(8, D)
    o3r = ar.f32(8, D)
    s2b = ar.f32(8, D)
    ysq5 = ar.f32(8, D)
    st5 = ar.f32(4, 8)
    outb = ar.f32(8, D)
    ADDr = ("ve", 0)
    outc = 0
    for g in range(2):
        P.waits("sync", [ADDr])
        P.dma("sync", x1r,
              x1_d[8 * g:8 * g + 8].rearrange("j p d -> p j d"), ltsem[g % 2])
        P.waits("gpsimd", [ADDr])
        grec = P.dma("gpsimd", o3r,
                     o3_d[8 * g:8 * g + 8].rearrange("j p d -> p j d"), "gc")
        P.wait("vector", ltsem[g % 2])
        P.wait("vector", *grec)
        if outc:
            P.wait("vector", "outst", outc)
        P.do("vector", lambda v: v.tensor_add(s2b, x1r, o3r), "ve")
        ADDr = P.rec("ve")
        P.do("vector", lambda v: v.reduce_sum(
            st5[:, 0, 0:8], s2b, axis=mybir.AxisListType.X), "ve")
        S1M = P.rec("ve")
        P.waits("scalar", [S1M])
        P.do("scalar", lambda e: e.activation(ysq5, s2b, AF.Square), "sc")
        SQ = P.rec("sc")
        P.waits("vector", [SQ])
        P.do("vector", lambda v: v.reduce_sum(
            st5[:, 1, 0:8], ysq5, axis=mybir.AxisListType.X), "ve")
        P.do("vector", lambda v: v.tensor_scalar(
            st5[:, 0, 0:8], st5[:, 0, 0:8], 1.0 / D, 0.0, ALU.mult,
            ALU.add), "ve")
        P.do("vector", lambda v: v.tensor_scalar(
            st5[:, 1, 0:8], st5[:, 1, 0:8], 1.0 / D, 0.0, ALU.mult,
            ALU.add), "ve")
        P.do("vector", lambda v: v.tensor_mul(
            st5[:, 2, 0:8], st5[:, 0, 0:8], st5[:, 0, 0:8]), "ve")
        P.do("vector", lambda v: v.tensor_sub(
            st5[:, 1, 0:8], st5[:, 1, 0:8], st5[:, 2, 0:8]), "ve")
        V5 = P.rec("ve")
        P.waits("scalar", [V5])
        P.do("scalar", lambda e: e.activation(
            st5[:, 3, 0:8], st5[:, 1, 0:8], AF.Ln, bias=eps5), "sc")
        P.do("scalar", lambda e: e.activation(
            st5[:, 3, 0:8], st5[:, 3, 0:8], AF.Exp, scale=-0.5), "sc")
        R5 = P.rec("sc")
        P.waits("vector", [R5])
        P.do("vector", lambda v: v.tensor_sub(
            outb, s2b,
            st5[:, 0, 0:8].unsqueeze(2).broadcast_to((128, 8, D))), "ve")
        P.do("vector", lambda v: v.tensor_mul(
            outb, outb,
            st5[:, 3, 0:8].unsqueeze(2).broadcast_to((128, 8, D))), "ve")
        OUTr = P.rec("ve")
        P.waits("sync", [OUTr])
        P.dma("sync", out_hbm[1024 * g:1024 * (g + 1), :].rearrange(
            "(j p) d -> p j d", p=128), outb, "outst")
        outc = P.val("outst")
    return False



def host_prep(inputs):
    x = np.asarray(inputs["L0_attn_x"], np.float32).reshape(16384, D)
    rW = np.asarray(inputs["L0_attn_rW"], np.float32)
    kW = np.asarray(inputs["L0_attn_kW"], np.float32)
    vW = np.asarray(inputs["L0_attn_vW"], np.float32)
    gW = np.asarray(inputs["L0_attn_gW"], np.float32)
    oW = np.asarray(inputs["L0_attn_oW"], np.float32)
    wt_all = np.concatenate([rW.T, kW.T, vW.T, gW.T], axis=1)
    wtt = np.ascontiguousarray(
        wt_all.reshape(8, 128, 4, 1024).transpose(2, 1, 0, 3).reshape(
            4, 128, 8 * 1024))
    oWtt = np.ascontiguousarray(
        oW.T.reshape(8, 128, 1024).transpose(1, 0, 2).reshape(128, 8 * 1024))
    las, lbs, lls = [], [], []
    for p in ("r", "k", "v", "g", "d"):
        las.append(np.asarray(inputs[f"L0_attn_{p}_la"], np.float32))
        lbs.append(np.asarray(inputs[f"L0_attn_{p}_lb"], np.float32))
        lls.append(np.asarray(inputs[f"L0_attn_{p}_ll"], np.float32))
    laT = np.zeros((D, 384), np.float32)
    laT[:, :320] = np.concatenate(las, axis=0).T
    lbT = np.zeros((128, 3, D), np.float32)
    for pi in range(5):
        q, base = pi // 2, 64 * (pi % 2)
        lbT[base:base + 64, q, :] = lbs[pi].T
    ll_cols = np.ascontiguousarray(
        np.stack([v.reshape(8, 128).T for v in lls], axis=1))
    xw_col = np.ascontiguousarray(
        np.asarray(inputs["L0_attn_xw"], np.float32).reshape(8, 128).T)
    u_col = np.ascontiguousarray(
        np.asarray(inputs["L0_attn_u"], np.float32).reshape(8, 128).T)
    dll_col = np.ascontiguousarray(lls[4].reshape(8, 128).T)
    init_state = np.asarray(inputs["L0_attn_init_state"], np.float32)
    S0 = np.zeros((128, 512), np.float32)
    for h in range(H):
        S0[64 * (h % 2):64 * (h % 2) + 64,
           64 * (h // 2):64 * (h // 2) + 64] = init_state[h]
    ident = np.eye(128, dtype=BF)
    sM = np.zeros((32, 32), np.float32)
    si, ti = np.meshgrid(np.arange(32), np.arange(32), indexing="ij")
    sM[(ti - si) > BAND] = np.exp(QOFF)
    mscaleA = np.ascontiguousarray(np.tile(sM, (4, 16)))
    bmask = np.ones((128, SEG), np.float32)
    bmask[:, ::CH] = 0.0
    ind16 = np.zeros((128, 8, 16), BF)
    for m in range(8):
        ind16[0:64, m, 2 * m] = 1
        ind16[64:128, m, 2 * m + 1] = 1
    ln_w = np.asarray(inputs["L0_attn_ln_w"], np.float32)
    ln_b = np.asarray(inputs["L0_attn_ln_b"], np.float32)
    lnw_tile = np.ascontiguousarray(np.tile(ln_w, (128, 16)))
    lnb_tile = np.ascontiguousarray(np.tile(ln_b, (128, 16)))
    w3s_tile = np.ascontiguousarray(
        np.tile(np.asarray(inputs["L0_ffn_w3_s"], np.float32), (128, 1)))
    s_cols = np.ascontiguousarray(np.stack(
        [np.asarray(inputs[f"L0_ffn_{n}"], np.float32).reshape(32, 128).T
         for n in ("w1_s", "w2_s", "h1_s", "h2_s")], axis=1))
    def tile_oc(wT, kk, noc, o):
        K, O = wT.shape
        return np.ascontiguousarray(
            wT.reshape(kk, 128, noc, o).transpose(2, 1, 0, 3).reshape(
                noc, 128, kk * o))

    w1t = np.asarray(inputs["L0_ffn_w1_w"], np.float32).T
    w2t = np.asarray(inputs["L0_ffn_w2_w"], np.float32).T
    h1t = np.asarray(inputs["L0_ffn_h1_w"], np.float32).T
    h2t = np.asarray(inputs["L0_ffn_h2_w"], np.float32).T
    w3t = np.asarray(inputs["L0_ffn_w3_w"], np.float32).T
    w12t = np.ascontiguousarray(np.concatenate(
        [tile_oc(w1t, 8, 32, 128).reshape(32, 128, 8, 128),
         tile_oc(w2t, 8, 32, 128).reshape(32, 128, 8, 128)],
        axis=2).reshape(32, 128, 16 * 128))
    h1tt = tile_oc(h1t, 32, 32, 128)
    h2tt = tile_oc(h2t, 32, 32, 128)
    w3tt = tile_oc(w3t, 32, 4, 256)

    shared = dict(wtt=wtt, oWtt=oWtt, laT=laT, lbT=lbT, ll_cols=ll_cols,
                  xw_col=xw_col, dll_col=dll_col, u_col=u_col, S0=S0,
                  ident=ident, mscaleA=mscaleA, bmask=bmask, ind16=ind16,
                  lnw_tile=lnw_tile, lnb_tile=lnb_tile, w3s_tile=w3s_tile,
                  s_cols=s_cols, w12t=w12t, h1tt=h1tt, h2tt=h2tt, w3tt=w3tt)
    in_maps = []
    for c in range(NCORE):
        im = dict(shared)
        im["x_hbm"] = np.ascontiguousarray(x[c * TC:(c + 1) * TC])
        im["xprev"] = (np.zeros((1, D), np.float32) if c == 0
                       else np.ascontiguousarray(x[c * TC - 1:c * TC]))
        pmv = np.zeros((128, 8), np.float32)
        pmv[:, :c] = 1.0
        im["pm"] = pmv
        im["pm1"] = np.ascontiguousarray(1.0 - pmv)
        in_maps.append(im)
    return in_maps


def kernel(**inputs):
    in_maps = host_prep(inputs)
    nc = build()
    res = run_bass_kernel_spmd(nc, in_maps, core_ids=list(range(NCORE)))
    out = np.concatenate([res.results[i]["out"] for i in range(NCORE)], axis=0)
    return out.reshape(1, 16384, 1024)


# version 10
# speedup vs baseline: 1.0537x; 1.0537x over previous
import numpy as np
import ml_dtypes
import concourse.bass as bass
import concourse.mybir as mybir
from concourse.bass_utils import run_bass_kernel_spmd
from contextlib import ExitStack

F32 = mybir.dt.float32
BF16 = mybir.dt.bfloat16
AF = mybir.ActivationFunctionType
ALU = mybir.AluOpType
BF = ml_dtypes.bfloat16

NCORE = 8
TC = 2048
SEG = 512
NSEG = 4
NTILE = 16
CH = 32
BAND = 7
QOFF = 44.0
THRESH = 126.0
BIGNEG = 1e9
D = 1024
H = 16
HD = 64
DH = 4096
PADC = 40
PAD0 = 8
EPS = 1e-5
SEGW = 8 * 16 * PADC
ENGS = ("sync", "gpsimd", "vector", "scalar", "tensor")
DMASEMS = {"sync": ("ldc", "st", "ltA", "ltB"),
           "gpsimd": ("gx", "gc", "lwA", "lwB", "fwA", "fwB"),
           "scalar": ("scA", "scB"), "vector": (), "tensor": ()}


class Prog:
    def __init__(self, nc, ctx):
        self.nc = nc
        self.ops = {e: [] for e in ENGS}
        self.semh = {}
        self.semc = {}
        self.ctx = ctx
        self.trace = []
        self.last_pe_key = None
        self._pe_key_hold = None
        self.phases = []

    def phase(self, name):
        self.phases.append((name, {e: len(self.ops[e]) for e in ENGS}))

    def sem(self, name):
        self.semh[name] = self.ctx.enter_context(self.nc.semaphore(name))
        self.semc[name] = 0

    def pe_mm(self, make, key, sem=None, inc=1):
        if self.last_pe_key != key:
            self.ops["tensor"].append(lambda e: e.nop())
        self.last_pe_key = key
        self.do("tensor", make, sem, inc)
        self.last_pe_key = key

    def do(self, eng, make, sem=None, inc=1, fence=None):
        if eng == "tensor":
            self.last_pe_key = None
        if fence is None:
            fence = eng in ("vector", "scalar", "gpsimd")
        if sem is None:
            if fence:
                self.ops[eng].append(
                    lambda e, mk=make: (mk(e), e.drain()))
            else:
                self.ops[eng].append(make)
        else:
            self.semc[sem] += inc
            self.trace.append((eng, "op+inc", sem, self.semc[sem]))
            h = self.semh[sem]
            if fence:
                self.ops[eng].append(
                    lambda e, mk=make, hh=h, ii=inc: (
                        mk(e).then_inc(hh, ii), e.drain()))
            else:
                self.ops[eng].append(
                    lambda e, mk=make, hh=h, ii=inc: mk(e).then_inc(hh, ii))

    def dma(self, eng, out, in_, sem, slow=False):
        self.semc[sem] += 16
        self.trace.append((eng, "dma+inc", sem, self.semc[sem]))
        h = self.semh[sem]
        self.ops[eng].append(
            lambda e, o=out, i=in_, hh=h, sl=slow: e.dma_start(
                out=o, in_=i, allow_slow_non_contiguous=sl).then_inc(hh, 16))
        return (sem, self.semc[sem])

    def wait(self, eng, sem, val=None):
        v = self.semc[sem] if val is None else val
        h = self.semh[sem]
        self.trace.append((eng, "wait", sem, v))
        self.ops[eng].append(lambda e, hh=h, vv=v: e.wait_ge(hh, vv))

    def waits(self, eng, recs):
        seen = {}
        for sem, v in recs:
            seen[sem] = max(seen.get(sem, -1), v)
        for sem, v in seen.items():
            if v > 0:
                self.wait(eng, sem, v)

    def rec(self, sem):
        return (sem, self.semc[sem])

    def val(self, sem):
        return self.semc[sem]

    def barrier(self):
        esem = {"sync": "bsy", "gpsimd": "bgp", "vector": "bve",
                "scalar": "bsc", "tensor": "bpe"}
        for e in ENGS:
            for ds in DMASEMS[e]:
                self.wait(e, ds)
            self.semc[esem[e]] += 1
            self.trace.append((e, "barrier-inc", esem[e], self.semc[esem[e]]))
            h = self.semh[esem[e]]
            self.ops[e].append(lambda x, hh=h: x.nop().then_inc(hh, 1))
        for e in ENGS:
            for o in ENGS:
                if o != e:
                    self.wait(e, esem[o])


class Arena:
    def __init__(self, nc, ctx, name, nwords):
        self.t = ctx.enter_context(nc.sbuf_tensor(name, [128, nwords], F32))
        self.nwords = nwords
        self.off = 0
        self.peak = 0

    def _alloc(self, n, align=16):
        o = (self.off + align - 1) // align * align
        self.off = o + n
        self.peak = max(self.peak, self.off)
        assert self.off <= self.nwords, f"arena overflow: {self.off} > {self.nwords}"
        return o

    def _shape(self, ap, shape):
        if len(shape) > 1:
            names = [f"d{i}" for i in range(len(shape))]
            ap = ap.rearrange("p (" + " ".join(names) + ") -> p " + " ".join(names),
                              **{n: s for n, s in zip(names, shape)})
        return ap

    def f32(self, *shape):
        n = int(np.prod(shape))
        o = self._alloc(n)
        return self._shape(self.t.ap()[:, o:o + n], shape)

    def bf16(self, *shape):
        n = int(np.prod(shape))
        nw = (n + 1) // 2
        o = self._alloc(nw)
        return self._shape(self.t.ap()[:, o:o + nw].bitcast(BF16), shape)

    def mark(self):
        return self.off

    def reset(self, m):
        self.off = m


def build(dbg=(), stop_after=None, nseg=NSEG):
    nc = bass.Bass("TRN2", target_bir_lowering=False, detect_race_conditions=False)

    def din(name, shape, dt=F32):
        return nc.dram_tensor(name, list(shape), dt, kind="ExternalInput").ap()

    x_hbm = din("x_hbm", (TC, D))
    xprev = din("xprev", (1, D))
    wtt = din("wtt", (4, 128, 8 * D))
    oWtt = din("oWtt", (128, 8 * D))
    laT = din("laT", (D, 384))
    lbT = din("lbT", (128, 3, D))
    ll_cols = din("ll_cols", (128, 5, 8))
    xw_col = din("xw_col", (128, 8))
    dll_col = din("dll_col", (128, 8))
    u_col = din("u_col", (128, 8))
    S0_in = din("S0", (128, 512))
    pm_in = din("pm", (128, 8))
    pm1_in = din("pm1", (128, 8))
    ident_in = din("ident", (128, 128), BF16)
    mscale_in = din("mscaleA", (128, 512))
    bmask_in = din("bmask", (128, SEG))
    ind16_in = din("ind16", (128, 8, 16), BF16)
    lnw_in = din("lnw_tile", (128, D))
    lnb_in = din("lnb_tile", (128, D))
    w3s_in = din("w3s_tile", (128, D))
    scols_in = din("s_cols", (128, 4, 32))
    w12t = din("w12t", (32, 128, 16 * 128))
    h1tt = din("h1tt", (32, 128, 32 * 128))
    h2tt = din("h2tt", (32, 128, 32 * 128))
    w3tt = din("w3tt", (4, 128, 32 * 256))

    out_hbm = nc.dram_tensor("out", [TC, D], F32, kind="ExternalOutput").ap()

    r_d = nc.dram_tensor("r_d", [NSEG, 4, 128, SEGW // 4], BF16,
                         kind="Internal").ap()
    k_d = nc.dram_tensor("k_d", [NSEG, 4, 128, SEGW // 4], BF16,
                         kind="Internal").ap()
    c_d = nc.dram_tensor("c_d", [NSEG, 4, 128, SEGW // 4], F32,
                         kind="Internal").ap()
    v_d = nc.dram_tensor("v_d", [NTILE, 128, D], BF16, kind="Internal").ap()
    g_d = nc.dram_tensor("g_d", [NTILE, 128, D], BF16, kind="Internal").ap()
    M_d = nc.dram_tensor("M_d", [64, 128, 512], F32, kind="Internal").ap()
    x1_d = nc.dram_tensor("x1_d", [NTILE, 128, D], F32, kind="Internal").ap()
    o3_d = nc.dram_tensor("o3_d", [NTILE, 128, D], F32, kind="Internal").ap()
    ha_d = nc.dram_tensor("ha_d", [32, 128, TC], BF16, kind="Internal").ap()
    hb_d = nc.dram_tensor("hb_d", [32, 128, TC], BF16, kind="Internal").ap()
    ab_scr = nc.dram_tensor("ab_scr", [2, 128 * 128 * 16], BF16, kind="Internal").ap()
    cc_in = nc.dram_tensor("cc_in", [128, 520], F32, kind="Internal").ap()
    cc_out = nc.dram_tensor("cc_out", [128 * NCORE, 520], F32, kind="Internal",
                            addr_space="Shared").ap()

    dbg_out = {}
    for name, shape, dt in dbg:
        dbg_out[name] = nc.dram_tensor("dbg_" + name, list(shape), dt,
                                       kind="ExternalOutput").ap()

    with ExitStack() as ctx:
        ar = Arena(nc, ctx, "arena", 53200)
        pall = ctx.enter_context(nc.psum_tensor("pall", [128, 4096], F32))

        def pbank(i, n=512):
            return pall[:, 512 * i:512 * i + n]

        def pbank_bf(i, n):
            return pall[:, 512 * i:512 * i + (n + 1) // 2].bitcast(BF16)

        P = Prog(nc, ctx)
        globals()["LAST_PROG"] = P
        for s in ("ldc", "gx", "gc", "lwA", "lwB", "ltA", "ltB", "scA", "scB",
                  "fwA", "fwB", "pe", "ve", "sc", "gp", "st", "cc", "outst",
                  "bsy", "bgp", "bve", "bsc", "bpe"):
            P.sem(s)

        def tap(name, ap):
            if name in dbg_out:
                P.barrier()
                rec = P.dma("sync", dbg_out[name], ap, "st")
                P.wait("sync", *rec)
                P.barrier()

        P.phase("init")
        ident_sb = ar.bf16(128)
        mscale_sb = ar.f32(512)
        bmask_sb = ar.f32(SEG)
        ind16_sb = ar.bf16(8, 16)
        llc_sb = ar.f32(5, 8)
        xwc_sb = ar.f32(8)
        dllc_sb = ar.f32(8)
        uc_sb = ar.f32(8)
        pm_sb = ar.f32(8)
        pm1_sb = ar.f32(8)
        Spp = ar.f32(2, 512)
        Srun_sb = ar.f32(512)
        Drun_sb = ar.f32(8)
        D_all = ar.f32(64, 8)
        cend_all = ar.f32(64, 8)
        mr_all = ar.f32(64, 8)
        mk_all = ar.f32(64, 8)
        lnw_sb = ar.f32(D)
        lnb_sb = ar.f32(D)
        w3s_sb = ar.f32(D)
        scol_sb = ar.f32(4, 32)
        mb_sb = ar.f32(16, 8)
        PERSIST = ar.mark()

        for dst, src in ((ident_sb, ident_in), (mscale_sb, mscale_in),
                         (bmask_sb, bmask_in), (ind16_sb, ind16_in),
                         (llc_sb, ll_cols), (xwc_sb, xw_col), (dllc_sb, dll_col),
                         (uc_sb, u_col), (pm_sb, pm_in), (pm1_sb, pm1_in),
                         (Spp[:, 0], S0_in), (lnw_sb, lnw_in), (lnb_sb, lnb_in),
                         (w3s_sb, w3s_in), (scol_sb, scols_in)):
            P.dma("sync", dst, src, "ldc")
        P.do("vector", lambda v: v.memset(Srun_sb, 0.0), None)
        P.do("vector", lambda v: v.memset(Drun_sb, 1.0), None)

        ATW0 = ar.mark()
        wts = [ar.bf16(8, D), ar.bf16(8, D)]
        la_sb = ar.bf16(8, 384)
        lb_sb = ar.bf16(3, D)
        P.dma("gpsimd", la_sb, laT.rearrange("(m p) o -> p m o", p=128), "gc")
        P.dma("gpsimd", lb_sb, lbT, "gc")
        wt_free = {0: ("pe", 0), 1: ("pe", 0)}
        wsem = {0: "lwA", 1: "lwB"}

        x_bf = ar.bf16(4, D)
        x_fm = ar.bf16(8, 513)
        dx_fm = ar.bf16(8, SEG)
        t1_fm = ar.bf16(8, SEG)
        inp_fm = ar.bf16(8, SEG)
        zt_sb = ar.bf16(3, SEG)
        what_sb = ar.f32(SEG)
        cscr_sb = ar.f32(SEG)
        chat_sb = ar.f32(8, 16, PADC)
        r_fm = ar.bf16(8, 16, PADC)
        k_fm = ar.bf16(8, 16, PADC)
        v_tm = ar.bf16(4, D)
        g_tm = ar.bf16(4, D)
        ek_sb = ar.bf16(16, 32)
        ktr_scr = ar.bf16(2, 128)
        ktm_sb = ar.bf16(4, D)
        M_sb = ar.f32(2, 512)
        P1_END = ar.mark()

        P.dma("gpsimd", x_fm[:, :, 0:1],
              xprev.rearrange("o (m p) -> p m o", p=128), "gc", slow=True)
        P.do("vector", lambda v: v.memset(
            x_bf.rearrange("p a d -> p (a d)")[:, 0:2048], 0.0), "ve")
        ZED = P.rec("ve")
        P.waits("gpsimd", [ZED])
        for i in range(2):
            P.dma("gpsimd", ab_scr[i].rearrange("(p x) -> p x", p=128),
                  x_bf.rearrange("p a d -> p (a d)")[:, 0:2048], "gc")

        P.barrier()

        PORD = (4, 1, 2, 0, 3)
        proj_col = {0: 0, 1: D, 2: 2 * D, 3: 3 * D}

        for s in range(nseg):
            P.phase(f"p1s{s}")
            if s > 0:
                P.do("vector", lambda v: v.tensor_copy(
                    x_fm[:, :, 0:1], x_fm[:, :, 512:513]), "ve")
            BCOL = P.rec("ve")
            XBF = P.dma("gpsimd", x_bf,
                        x_hbm[s * SEG:(s + 1) * SEG, :].rearrange(
                            "(a p) d -> p a d", p=128), "gx")
            P.do("vector", lambda v: v.memset(
                chat_sb.rearrange("p m c w -> p (m c w)"), 0.0), None)
            P.do("vector", lambda v: v.memset(
                k_fm.rearrange("p m c w -> p (m c w)"), 0.0), None)
            P.do("vector", lambda v: v.memset(
                r_fm.rearrange("p m c w -> p (m c w)"), 0.0), "ve")
            PADZ = P.rec("ve")

            P.wait("tensor", "gx")
            P.wait("tensor", "gc")
            P.wait("tensor", "ldc")
            P.waits("tensor", [BCOL])
            ev_at = {}
            for i, (a, m) in enumerate([(a, m) for a in range(4) for m in range(8)]):
                b = 2 + (i % 2)
                if i >= 2:
                    P.waits("tensor", [ev_at[i - 2]])
                P.do("tensor", lambda t, a=a, m=m, b=b: t.transpose(
                    pbank_bf(b, 128)[:, 0:128],
                    x_bf[:, a, 128 * m:128 * (m + 1)], ident_sb), "pe")
                PEh = P.rec("pe")
                P.waits("vector", [PEh])
                P.do("vector", lambda v, a=a, m=m, b=b: v.tensor_copy(
                    x_fm[:, m, 1 + 128 * a:1 + 128 * (a + 1)],
                    pbank_bf(b, 128)[:, 0:128]), "ve")
                ev_at[i] = P.rec("ve")
            P.do("vector", lambda v: v.tensor_sub(
                dx_fm, x_fm[:, :, 0:512], x_fm[:, :, 1:513]), None)
            for m in range(8):
                P.do("vector", lambda v, m=m: v.scalar_tensor_tensor(
                    t1_fm[:, m, :], dx_fm[:, m, :], xwc_sb[:, m:m + 1],
                    x_fm[:, m, 1:513], ALU.mult, ALU.add), None)
            P.do("vector", lambda v: v.nop(), "ve")
            LERP = P.rec("ve")
            if stop_after == "xprep":
                P.waits("sync", [LERP])
                tap("xfm0", x_fm.rearrange("p m t -> p (m t)"))
                tap("dx0", dx_fm.rearrange("p m t -> p (m t)"))
                tap("lerp0", t1_fm.rearrange("p m t -> p (m t)"))
                _close(P, nc, ctx, [])
                return nc
            if s == 0:
                P.waits("sync", [LERP])
                tap("xfm0", x_fm.rearrange("p m t -> p (m t)"))
                tap("dx0", dx_fm.rearrange("p m t -> p (m t)"))
                tap("lerp0", t1_fm.rearrange("p m t -> p (m t)"))

            P.waits("tensor", [LERP])
            zt_done = {}
            for q in range(3):
                b = q % 2
                if q >= 2:
                    P.waits("tensor", [zt_done[q - 2]])
                for kk in range(8):
                    P.do("tensor", lambda t, q=q, kk=kk, b=b: t.matmul(
                        pbank(b), la_sb[:, kk, 128 * q:128 * (q + 1)],
                        t1_fm[:, kk, :], start=(kk == 0), stop=(kk == 7)), "pe")
                PEq = P.rec("pe")
                P.waits("scalar", [PEq])
                P.do("scalar", lambda e, q=q, b=b: e.activation(
                    zt_sb[:, q, :], pbank(b), AF.Tanh), "sc")
                zt_done[q] = P.rec("sc")
            if s == 0:
                P.waits("sync", [zt_done[2]])
                tap("zt0", zt_sb.rearrange("p q t -> p (q t)"))
            if stop_after == "lora":
                _close(P, nc, ctx, [])
                return nc

            last_evac = {0: zt_done[2], 1: zt_done[1], 4: ("sc", 0), 5: ("sc", 0)}
            scan_done = ("ve", 0)
            inp_done = ("gp", 0)
            kstore = None
            vprj = ("sc", 0)
            DDONE = ("sc", 0)
            slab_of = {1: 0, 2: 1, 0: 0, 3: 1}

            def wt_load(pi):
                sl = slab_of[pi]
                P.waits("gpsimd", [wt_free[sl]])
                P.dma("gpsimd", wts[sl].rearrange("p m o -> p (m o)"),
                      wtt[pi], wsem[sl])

            for pi in (1, 2):
                wt_load(pi)

            for pi in PORD:
                q, base = pi // 2, 64 * (pi % 2)
                P.waits("tensor", [zt_done[q], inp_done])
                for m in range(8):
                    b = m % 2
                    P.waits("tensor", [last_evac[b]])
                    P.do("tensor", lambda t, m=m, b=b, q=q, base=base: t.matmul(
                        pbank(b), lb_sb[base:base + 64, q, 128 * m:128 * (m + 1)],
                        zt_sb[base:base + 64, q, :], start=True, stop=True), "pe")
                    PEm = P.rec("pe")
                    P.waits("vector", [PEm])
                    P.do("vector", lambda v, m=m, b=b, pi=pi: v.scalar_tensor_tensor(
                        t1_fm[:, m, :], pbank(b), llc_sb[:, pi, m:m + 1],
                        dx_fm[:, m, :], ALU.add, ALU.mult), "ve")
                    last_evac[b] = P.rec("ve")
                    P.waits("gpsimd", [last_evac[b]])
                    P.do("gpsimd", lambda g, m=m: g.tensor_add(
                        inp_fm[:, m, :], t1_fm[:, m, :], x_fm[:, m, 1:513]), "gp")
                INP = P.rec("gp")
                inp_done = INP
                if s == 0 and pi == 4:
                    P.waits("sync", [INP])
                    tap("inpd0", inp_fm.rearrange("p m t -> p (m t)"))

                if pi == 4:
                    P.waits("tensor", [INP, ev_at[31]])
                    for kk in range(8):
                        P.do("tensor", lambda t, kk=kk: t.matmul(
                            pbank(2)[0:64, :], la_sb[:, kk, 256:320],
                            inp_fm[:, kk, :], start=(kk == 0), stop=(kk == 7)), "pe")
                    PEzd = P.rec("pe")
                    P.waits("scalar", [PEzd])
                    P.do("scalar", lambda e: e.activation(
                        zt_sb[0:64, 2, :], pbank(2)[0:64, :], AF.Tanh), "sc")
                    ZD = P.rec("sc")
                    P.waits("tensor", [ZD])
                    for m in range(8):
                        b = m % 2
                        P.waits("tensor", [last_evac[b]])
                        P.do("tensor", lambda t, m=m, b=b: t.matmul(
                            pbank(b), lb_sb[0:64, 2, 128 * m:128 * (m + 1)],
                            zt_sb[0:64, 2, :], start=True, stop=True), "pe")
                        PEd = P.rec("pe")
                        P.waits("scalar", [PEd, scan_done, PADZ])
                        P.do("scalar", lambda e, m=m, b=b: e.activation(
                            what_sb, pbank(b), AF.Exp, bias=dllc_sb[:, m:m + 1]),
                            "sc")
                        last_evac[b] = P.rec("sc")
                        P.waits("vector", [last_evac[b]])
                        P.do("vector", lambda v: v.tensor_tensor_scan(
                            cscr_sb, bmask_sb, what_sb,
                            0.0, ALU.mult, ALU.add), None)
                        P.do("vector", lambda v, m=m: v.tensor_copy(
                            chat_sb[:, m, :, PAD0:PADC],
                            cscr_sb.rearrange("p (c w) -> p c w", w=32)), "ve")
                        scan_done = P.rec("ve")
                    P.do("vector", lambda v, s=s: v.tensor_copy(
                        cend_all[:, 16 * s:16 * s + 16, :],
                        chat_sb[:, :, :, PADC - 1].transpose([0, 2, 1])), "ve")
                    CEND = P.rec("ve")
                    P.waits("scalar", [CEND])
                    P.do("scalar", lambda e, s=s: e.activation(
                        D_all[:, 16 * s:16 * s + 16, :],
                        cend_all[:, 16 * s:16 * s + 16, :], AF.Exp, scale=-1.0), "sc")
                    DDONE = P.rec("sc")
                    ce = cend_all[:, 16 * s:16 * s + 16, :].rearrange(
                        "p c m -> p (c m)")
                    mbv = mb_sb.rearrange("p c m -> p (c m)")
                    mrv = mr_all[:, 16 * s:16 * s + 16, :].rearrange(
                        "p c m -> p (c m)")
                    mkv = mk_all[:, 16 * s:16 * s + 16, :].rearrange(
                        "p c m -> p (c m)")
                    P.do("vector", lambda v, ce=ce, mbv=mbv: v.tensor_scalar(
                        mbv, ce, THRESH, None, ALU.is_gt), None)
                    P.do("vector", lambda v, ce=ce, mrv=mrv: v.tensor_scalar(
                        mrv, ce, 0.5, -QOFF / 2, ALU.mult, ALU.add), None)
                    P.do("vector", lambda v, mbv=mbv, mrv=mrv: v.scalar_tensor_tensor(
                        mrv, mbv, -BIGNEG, mrv, ALU.mult, ALU.add), None)
                    P.do("vector", lambda v, ce=ce, mkv=mkv: v.tensor_scalar(
                        mkv, ce, 0.5, QOFF / 2, ALU.mult, ALU.add), None)
                    P.do("vector", lambda v, mbv=mbv, mkv=mkv: v.scalar_tensor_tensor(
                        mkv, mbv, BIGNEG, mkv, ALU.mult, ALU.add), "ve")
                    if s == 0:
                        P.waits("sync", [scan_done, P.rec("ve"), DDONE])
                        tap("chat0", chat_sb.rearrange("p m c w -> p (m c w)"))
                    if stop_after == "dpath":
                        _close(P, nc, ctx, [])
                        return nc
                    continue

                if pi in (1, 0):
                    P.wait("tensor", wsem[slab_of[pi]])
                    dst = k_fm if pi == 1 else r_fm
                    for m in range(8):
                        b = m % 2
                        P.waits("tensor", [last_evac[b], INP, PADZ])
                        for kk in range(8):
                            P.do("tensor", lambda t, m=m, kk=kk, b=b,
                                 sl=slab_of[pi]: t.matmul(
                                pbank(b),
                                wts[sl][:, kk, 128 * m:128 * (m + 1)],
                                inp_fm[:, kk, :], start=(kk == 0), stop=(kk == 7)),
                                "pe")
                        PEp = P.rec("pe")
                        P.waits("scalar", [PEp])
                        P.do("scalar", lambda e, m=m, b=b, dst=dst: e.copy(
                            dst[:, m, :, PAD0:PADC],
                            pbank(b).rearrange("p (c w) -> p c w", w=32)), "sc")
                        last_evac[b] = P.rec("sc")
                    wt_free[slab_of[pi]] = P.rec("pe")
                    nxt = {1: 0, 2: 3}.get(pi)
                    if nxt is not None:
                        wt_load(nxt)
                    PRJ = P.rec("sc")
                    P.waits("sync", [PRJ])
                    dd = k_d if pi == 1 else r_d
                    for al in range(4):
                        strec = P.dma("sync",
                                      dd[s, al].rearrange(
                                          "p (m c w) -> p m c w", m=8, w=PADC),
                                      dst[:, :, 4 * al:4 * al + 4, :], "st")
                    if pi == 1:
                        kstore = strec
                    if s == 0:
                        tap("kfm0" if pi == 1 else "rfm0",
                            dst.rearrange("p m c w -> p (m c w)"))
                else:
                    P.wait("tensor", wsem[slab_of[pi]])
                    dst = v_tm if pi == 2 else g_tm
                    for a in range(4):
                        for n in range(2):
                            b = 4 + n
                            P.waits("tensor", [last_evac[b], INP])
                            for kk in range(8):
                                P.do("tensor",
                                     lambda t, a=a, n=n, kk=kk, b=b,
                                     sl=slab_of[pi]: t.matmul(
                                         pbank(b),
                                         inp_fm[:, kk, 128 * a:128 * (a + 1)],
                                         wts[sl][:, kk, 512 * n:512 * (n + 1)],
                                         start=(kk == 0), stop=(kk == 7)), "pe")
                            PEp = P.rec("pe")
                            P.waits("scalar", [PEp])
                            P.do("scalar", lambda e, a=a, n=n, b=b, dst=dst: e.copy(
                                dst[:, a, 512 * n:512 * (n + 1)], pbank(b)), "sc")
                            last_evac[b] = P.rec("sc")
                    wt_free[slab_of[pi]] = P.rec("pe")
                    nxt = {1: 0, 2: 3}.get(pi)
                    if nxt is not None:
                        wt_load(nxt)
                    PRJ = P.rec("sc")
                    if pi == 2:
                        vprj = PRJ
                    P.waits("sync", [PRJ])
                    dd = v_d if pi == 2 else g_d
                    for a in range(4):
                        P.dma("sync", dd[4 * s + a], dst[:, a, :], "st")
                    if s == 0:
                        tap("vtm0" if pi == 2 else "gtm0",
                            dst.rearrange("p a d -> p (a d)"))

            P.waits("sync", [scan_done])
            for al in range(4):
                P.dma("sync",
                      c_d[s, al].rearrange("p (m c w) -> p m c w", m=8, w=PADC),
                      chat_sb[:, :, 4 * al:4 * al + 4, :], "st")

            P.wait("vector", "st")
            for m in range(8):
                P.do("vector", lambda v, m=m: v.tensor_sub(
                    what_sb.rearrange("p (c w) -> p c w", w=32),
                    chat_sb[:, m, :, PAD0:PADC],
                    chat_sb[:, m, :, PADC - 1:PADC].broadcast_to((128, 16, 32))),
                    "ve")
                EARG = P.rec("ve")
                P.waits("scalar", [EARG])
                P.do("scalar", lambda e: e.activation(
                    ek_sb, what_sb.rearrange("p (c w) -> p c w", w=32), AF.Exp),
                    "sc")
                EK = P.rec("sc")
                P.waits("vector", [EK])
                P.do("vector", lambda v, m=m: v.tensor_mul(
                    k_fm[:, m, :, PAD0:PADC], k_fm[:, m, :, PAD0:PADC], ek_sb),
                    "ve")
            KHAT = P.rec("ve")
            if s == 0:
                P.waits("sync", [KHAT])
                tap("khat0", k_fm.rearrange("p m c w -> p (m c w)"))
            if stop_after == "khat":
                _close(P, nc, ctx, [])
                return nc

            tr_done = {}
            stage_done = {}
            for i, (a, m) in enumerate([(a, m) for a in range(4) for m in range(8)]):
                b = 2 + (i % 2)
                j = i % 2
                if i >= 2:
                    P.waits("vector", [stage_done[i - 2]])
                P.do("vector", lambda v, a=a, m=m, j=j: v.tensor_copy(
                    ktr_scr[:, j, :].rearrange("p (c w) -> p c w", w=32),
                    k_fm[:, m, 4 * a:4 * a + 4, PAD0:PADC]), "ve")
                STG = P.rec("ve")
                if i >= 2:
                    P.waits("tensor", [tr_done[i - 2]])
                P.waits("tensor", [STG])
                P.do("tensor", lambda t, j=j, b=b: t.transpose(
                    pbank_bf(b, 128)[:, 0:128],
                    ktr_scr[:, j, :], ident_sb), "pe")
                PEt = P.rec("pe")
                stage_done[i] = PEt
                P.waits("vector", [PEt])
                P.do("vector", lambda v, a=a, m=m, b=b: v.tensor_copy(
                    ktm_sb[:, a, 128 * m:128 * (m + 1)],
                    pbank_bf(b, 128)[:, 0:128]), "ve")
                tr_done[i] = P.rec("ve")
            KTM = P.rec("ve")
            if s == 0:
                P.waits("sync", [KTM])
                tap("ktm0", ktm_sb.rearrange("p a d -> p (a d)"))
            if stop_after == "ktm":
                _close(P, nc, ctx, [])
                return nc

            m_evac = {0: ("sc", 0), 1: ("sc", 0)}
            m_store = {0: ("st", 0), 1: ("st", 0)}
            sum_done = ("gp", 0)
            for cl in range(16):
                a, ccc = cl // 4, cl % 4
                P.waits("tensor", [KTM, vprj, m_evac[cl % 2]])
                for h in range(16):
                    P.do("tensor", lambda t, a=a, ccc=ccc, h=h, pb=6 + cl % 2:
                         t.matmul(
                        pbank(pb)[64 * (h % 2):64 * (h % 2) + 64,
                                  64 * (h // 2):64 * (h // 2) + 64],
                        ktm_sb[32 * ccc:32 * ccc + 32, a,
                               128 * (h // 2) + 64 * (h % 2):
                               128 * (h // 2) + 64 * (h % 2) + 64],
                        v_tm[32 * ccc:32 * ccc + 32, a, 64 * h:64 * h + 64],
                        start=True, stop=True,
                        tile_position=(32 * ccc, 64 * (h % 2))), "pe")
                PEM = P.rec("pe")
                P.waits("scalar", [PEM, sum_done])
                if cl >= 2:
                    P.wait("scalar", "st")
                P.do("scalar", lambda e, cl=cl: e.copy(
                    M_sb[:, cl % 2, :], pbank(6 + cl % 2)), "sc")
                m_evac[cl % 2] = P.rec("sc")
                P.waits("sync", [m_evac[cl % 2]])
                m_store[cl % 2] = P.dma("sync", M_d[16 * s + cl],
                                        M_sb[:, cl % 2, :], "st")
                P.waits("gpsimd", [m_evac[cl % 2], DDONE])
                dcol = D_all[:, 16 * s + cl, :].unsqueeze(2).broadcast_to(
                    (128, 8, 64))
                P.do("gpsimd", lambda g, dcol=dcol: g.tensor_mul(
                    Srun_sb.rearrange("p (q j) -> p q j", j=64),
                    Srun_sb.rearrange("p (q j) -> p q j", j=64), dcol), None)
                P.do("gpsimd", lambda g, cl=cl: g.tensor_add(
                    Srun_sb, Srun_sb, M_sb[:, cl % 2, :]), None)
                P.do("gpsimd", lambda g, cl=cl, s=s: g.tensor_mul(
                    Drun_sb, Drun_sb, D_all[:, 16 * s + cl, :]), "gp")
                sum_done = P.rec("gp")
                if stop_after == "m1" and cl == 0:
                    break
                if stop_after == "m4" and cl == 3:
                    break

            P.barrier()
            if stop_after in ("m1", "m4", "m16"):
                _close(P, nc, ctx, [])
                return nc

        if stop_after == "seg":
            tap("Mrun", Srun_sb)
            tap("Drun", Drun_sb)
            tap("Dall", D_all.rearrange("p c m -> p (c m)"))
            tap("cend", cend_all.rearrange("p c m -> p (c m)"))
            tap("mra", mr_all.rearrange("p c m -> p (c m)"))
            tap("mka", mk_all.rearrange("p c m -> p (c m)"))
            _close(P, nc, ctx, [])
            return nc
        tap("Mrun", Srun_sb)
        tap("Drun", Drun_sb)
        tap("Dall", D_all.rearrange("p c m -> p (c m)"))
        tap("cend", cend_all.rearrange("p c m -> p (c m)"))
        tap("mra", mr_all.rearrange("p c m -> p (c m)"))
        tap("mka", mk_all.rearrange("p c m -> p (c m)"))

        P.phase("gather")
        P.dma("sync", cc_in[:, 0:512], Srun_sb, "st")
        cc2 = P.dma("sync", cc_in[:, 512:520], Drun_sb, "st")
        P.waits("gpsimd", [cc2])
        P.do("gpsimd", lambda g: g.collective_compute(
            "AllGather", ALU.bypass, replica_groups=[list(range(NCORE))],
            ins=[cc_in], outs=[cc_out]), "cc")
        ar.reset(ATW0)
        GTH0 = ar.mark()
        gath_sb = ar.f32(8, 520)
        deff_sb = ar.f32(8)
        tS_sb = ar.f32(512)
        P.waits("sync", [P.rec("cc")])
        P.dma("sync", gath_sb, cc_out.rearrange("(j p) w -> p j w", p=128),
              "ldc")
        tap("Spre", Spp[:, 0])
        P.wait("vector", "ldc")
        for j in range(NCORE - 1):
            P.do("vector", lambda v, j=j: v.tensor_scalar(
                deff_sb, gath_sb[:, j, 512:520], pm_sb[:, j:j + 1], None,
                ALU.mult), None)
            P.do("vector", lambda v, j=j: v.tensor_scalar(
                deff_sb, deff_sb, pm1_sb[:, j:j + 1], None, ALU.add), None)
            P.do("vector", lambda v: v.tensor_mul(
                tS_sb.rearrange("p (q j2) -> p q j2", j2=64),
                Spp[:, 0].rearrange("p (q j2) -> p q j2", j2=64),
                deff_sb.unsqueeze(2).broadcast_to((128, 8, 64))), None)
            P.do("vector", lambda v, j=j: v.scalar_tensor_tensor(
                Spp[:, 0].rearrange("p (q j2) -> p q j2", j2=64),
                gath_sb[:, j, 0:512].rearrange("p (q j2) -> p q j2", j2=64),
                pm_sb[:, j:j + 1],
                tS_sb.rearrange("p (q j2) -> p q j2", j2=64),
                ALU.mult, ALU.add), "ve")
        SIN = P.rec("ve")
        P.barrier()
        tap("deff", deff_sb)
        tap("gath", gath_sb.rearrange("p j w -> p (j w)"))
        tap("Sin", Spp[:, 0])

        if stop_after == "pass1":
            _close(P, nc, ctx, block_holder := [])
            return nc

        ar.reset(GTH0)
        oWt_sb = ar.bf16(8, D)
        P.dma("gpsimd", oWt_sb.rearrange("p m o -> p (m o)"), oWtt, "gc")
        rt2 = [ar.bf16(8, 4, PADC) for _ in range(2)]
        kt2 = [ar.bf16(8, 4, PADC) for _ in range(2)]
        ct2 = [ar.f32(8, 4, PADC) for _ in range(2)]
        vt2 = [ar.bf16(D) for _ in range(2)]
        gt2 = [ar.bf16(D) for _ in range(2)]
        xt2 = [ar.f32(D) for _ in range(2)]
        m42 = [ar.f32(4, 512) for _ in range(2)]
        ab2 = [ar.bf16(2048) for _ in range(2)]
        wtil_s = ar.f32(8, 4, 32)
        wstp_s = ar.f32(8, 4, PADC)
        wpro_s = ar.f32(8, 4, 32)
        er2 = [ar.bf16(8, 4, 32) for _ in range(2)]
        rst2 = [ar.bf16(8, 4, 32) for _ in range(2)]
        rA2 = [ar.bf16(8, 4, 32) for _ in range(2)]
        kA2 = [ar.bf16(8, 4, 32) for _ in range(2)]
        P82 = [ar.bf16(8, 8, 128) for _ in range(2)]
        Asb2 = [ar.bf16(512) for _ in range(2)]
        ad82 = [ar.bf16(8, 128) for _ in range(2)]
        strip = ar.bf16(8, 16)
        ysq = ar.f32(D)
        rl_sb = ar.f32(D)
        yt_sb = ar.f32(D)
        t2_sb = ar.f32(D)
        o_tm = ar.bf16(D)
        sg_sb = ar.bf16(D)
        o_fm = ar.bf16(8, 128)
        s1_sb = ar.f32(D)
        x1_sb = ar.f32(D)
        st16 = ar.f32(16, 4)
        Sbf = ar.bf16(2, 512)
        eps_col = ar.f32(1)
        P.do("vector", lambda v: v.memset(eps_col, EPS), "ve")
        PK2 = ar.mark()

        ltsem = {0: "ltA", 1: "ltB"}
        scsem = {0: "scA", 1: "scB"}

        def prefetch_tile(a):
            pi = a % 2
            sname = ltsem[pi]
            sgl, al = a // 4, a % 4
            P.dma("sync", rt2[pi].rearrange("p m c w -> p (m c w)"),
                  r_d[sgl, al], sname)
            P.dma("sync", kt2[pi].rearrange("p m c w -> p (m c w)"),
                  k_d[sgl, al], sname)
            P.dma("sync", ct2[pi].rearrange("p m c w -> p (m c w)"),
                  c_d[sgl, al], sname)
            P.dma("sync", vt2[pi], v_d[a], sname)
            P.dma("sync", gt2[pi], g_d[a], sname)
            P.dma("sync", xt2[pi], x_hbm[128 * a:128 * (a + 1), :], sname)
            P.dma("sync", m42[pi],
                  M_d[4 * a:4 * a + 4].rearrange("c p w -> p c w"), sname)

        P.waits("vector", [SIN])
        P.do("vector", lambda v: v.tensor_copy(Sbf[:, 0], Spp[:, 0]), "ve")
        SINB = P.rec("ve")
        spp_ready0 = SINB
        prefetch_tile(0)
        bank0_free = ("ve", 0)
        bank45_free = ("ve", 0)
        oW_free = ("ve", 0)
        strip_ev = ("ve", 0)
        ofm_ev = {}
        spp_ready = {0: spp_ready0, 1: ("ve", 0)}
        Z = ("ve", 0)
        OTM_prev = Z
        R1_prev = X1_prev = Z
        x1st_prev = 0
        tile_rel = {0: [], 1: []}
        ymm_hist = {0: ("pe", 0), 1: ("pe", 0)}
        ak2_hist = {0: ("sc", 0), 1: ("sc", 0)}
        pband_hist = {0: ("gp", 0), 1: ("gp", 0)}
        red_hist = {0: ("pe", 0), 1: ("pe", 0)}
        adb_hist = {0: ("ve", 0), 1: ("ve", 0)}
        trd_hist = {0: ("pe", 0), 1: ("pe", 0)}
        OFM_prev = ("ve", 0)
        def emit_tile(a):
            nonlocal bank0_free, bank45_free, oW_free, strip_ev
            nonlocal OTM_prev, OFM_prev
            nonlocal R1_prev, X1_prev, x1st_prev
            pi = a % 2
            wtil, wstp, wpro = wtil_s, wstp_s, wpro_s
            er_b, rst, rA, kA = er2[pi], rst2[pi], rA2[pi], kA2[pi]
            P8, A_sb, adbf8 = P82[pi], Asb2[pi], ad82[pi]
            RED_prev = red_hist[pi]
            if a + 1 < NTILE:
                P.waits("sync", tile_rel[(a + 1) % 2])
                prefetch_tile(a + 1)
            rt, kt, ct = rt2[pi], kt2[pi], ct2[pi]
            vt, gt, xt, m4 = vt2[pi], gt2[pi], xt2[pi], m42[pi]
            if stop_after == "p2z" and a == 0:
                return True
            P.wait("vector", ltsem[pi])
            P.wait("scalar", ltsem[pi])
            P.wait("tensor", ltsem[pi])
            ctv = ct[:, :, :, PAD0:PADC]
            ctp = ct[:, :, :, PAD0 - 1:PADC - 1]
            P.waits("vector", [ak2_hist[pi], ak2_hist[1 - pi],
                               ymm_hist[pi], pband_hist[pi]])
            P.do("vector", lambda v, wtil=wtil, ctv=ctv, ctp=ctp:
                 v.tensor_sub(wtil, ctv, ctp), "ve")
            P.do("vector", lambda v: v.memset(
                wstp.rearrange("p m c w -> p (m c w)"), 0.0), "ve")
            WV = P.rec("ve")
            P.waits("scalar", [WV, pband_hist[pi], pband_hist[1 - pi]])
            P.do("scalar", lambda e, wstp=wstp, wtil=wtil: e.activation(
                wstp[:, :, :, PAD0:PADC], wtil, AF.Exp, scale=-1.0), "sc")
            WSTP = P.rec("sc")
            P.do("scalar", lambda e: e.activation(
                er_b, ctp, AF.Exp, scale=-1.0), "sc")
            ER = P.rec("sc")
            P.waits("vector", [ER])
            P.do("vector", lambda v: v.tensor_mul(
                rst, rt[:, :, :, PAD0:PADC], er_b), "ve")
            mrv = mr_all[:, 4 * a:4 * a + 4, :].transpose([0, 2, 1]).unsqueeze(
                3).broadcast_to((128, 8, 4, 32))
            mkv = mk_all[:, 4 * a:4 * a + 4, :].transpose([0, 2, 1]).unsqueeze(
                3).broadcast_to((128, 8, 4, 32))
            P.do("vector", lambda v, mrv=mrv: v.tensor_sub(wtil, mrv, ctp), "ve")
            AR1 = P.rec("ve")
            P.waits("scalar", [AR1])
            P.do("scalar", lambda e: e.activation(er_b, wtil, AF.Exp), "sc")
            AR2 = P.rec("sc")
            P.waits("vector", [AR2])
            P.do("vector", lambda v: v.tensor_mul(
                rA, rt[:, :, :, PAD0:PADC], er_b), "ve")
            P.do("vector", lambda v, mkv=mkv: v.tensor_sub(wtil, ctv, mkv), "ve")
            AK1 = P.rec("ve")
            P.waits("scalar", [AK1])
            P.do("scalar", lambda e: e.activation(er_b, wtil, AF.Exp), "sc")
            AK2 = P.rec("sc")
            P.waits("vector", [AK2])
            P.do("vector", lambda v: v.tensor_mul(
                kA, kt[:, :, :, PAD0:PADC], er_b), "ve")
            PREPS = P.rec("ve")
            if stop_after == "p2pre" and a == 0:
                return True

            P.waits("tensor", [PREPS, bank0_free if a > 0 else ("ve", 0)])
            for par in (0, 1):
                for cc in range(4):
                    for h in range(par, 16, 2):
                        last = (par == 1 and cc == 3 and h == 15)
                        P.pe_mm(lambda t, cc=cc, h=h: t.matmul(
                            pbank(0)[32 * cc:32 * cc + 32, 32 * h:32 * h + 32],
                            kA[64 * (h % 2):64 * (h % 2) + 64, h // 2, cc, :],
                            rA[64 * (h % 2):64 * (h % 2) + 64, h // 2, cc, :],
                            start=True, stop=True,
                            tile_position=(64 * (h % 2), 32 * cc)),
                            key=("A", par), sem="pe" if last else None)
            AMM = P.rec("pe")
            if stop_after == "p2amm" and a == 0:
                return True
            P.waits("vector", [AMM, ymm_hist[pi]])
            P.do("vector", lambda v, A_sb=A_sb: v.tensor_mul(
                A_sb, pbank(0), mscale_sb), "ve")
            bank0_free = P.rec("ve")
            if stop_after == "p2a" and a == 0:
                tap("Asb0", A_sb)
                return True

            P.do("gpsimd", lambda g: g.memset(
                wpro.rearrange("p m c w -> p (m c w)"), 1.0), "gp")
            scat_cnt0 = P.val(scsem[pi])
            P.waits("gpsimd", [RED_prev])
            for dl in range(0, BAND + 1):
                pv = P8[:, dl, :, :].rearrange("p m (c w) -> p m c w", w=32)
                if dl >= 2:
                    if dl == 2:
                        P.waits("gpsimd", [WSTP])
                    P.do("gpsimd", lambda g, dl=dl: g.tensor_mul(
                        wpro, wpro,
                        wstp[:, :, :, PAD0 - (dl - 1):PADC - (dl - 1)]), "gp")
                if dl == 0:
                    ub = uc_sb.unsqueeze(2).unsqueeze(3).broadcast_to(
                        (128, 8, 4, 32))
                    P.do("gpsimd", lambda g, ub=ub, pv=pv: g.tensor_mul(
                        pv, rt[:, :, :, PAD0:PADC], ub), "gp")
                    P.do("gpsimd", lambda g, pv=pv: g.tensor_mul(
                        pv, pv, kt[:, :, :, PAD0:PADC]), "gp")
                elif dl == 1:
                    P.do("gpsimd", lambda g, pv=pv: g.tensor_mul(
                        pv, rt[:, :, :, PAD0:PADC],
                        kt[:, :, :, PAD0 - 1:PADC - 1]), "gp")
                else:
                    P.do("gpsimd", lambda g, pv=pv: g.tensor_mul(
                        pv, rt[:, :, :, PAD0:PADC], wpro), "gp")
                    P.do("gpsimd", lambda g, dl=dl, pv=pv: g.tensor_mul(
                        pv, pv, kt[:, :, :, PAD0 - dl:PADC - dl]), "gp")
            PBAND = P.rec("gp")
            if a == 0:
                P.waits("sync", [PBAND])
                tap("rt0", rt.rearrange("p m c w -> p (m c w)"))
                tap("kt0", kt.rearrange("p m c w -> p (m c w)"))
                tap("P0", P8[:, 0, :, :].rearrange("p m t -> p (m t)"))
                tap("uc0", uc_sb)
                if stop_after == "band0":
                    return True
            P.waits("tensor", [PBAND, adb_hist[pi], OFM_prev])
            for dl in range(8):
                for m in range(8):
                    P.do("tensor", lambda t, m=m, dl=dl: t.matmul(
                        pbank(1 + dl // 4)[0:16,
                                           128 * (dl % 4):128 * (dl % 4) + 128],
                        ind16_sb[:, m, :], P8[:, dl, m, :],
                        start=(m == 0), stop=(m == 7)),
                        "pe" if (dl == 7 and m == 7) else None)
            RED_prev = P.rec("pe")
            P.waits("vector", [RED_prev, trd_hist[pi]])
            for dl in range(8):
                P.do("vector", lambda v, dl=dl: v.tensor_copy(
                    adbf8[0:16, dl, :],
                    pbank(1 + dl // 4)[0:16,
                                       128 * (dl % 4):128 * (dl % 4) + 128]),
                    "ve")
            ADB = P.rec("ve")
            adb_hist[pi] = ADB
            P.waits("tensor", [ADB, strip_ev])
            for dl in range(8):
                P.do("tensor", lambda t, dl=dl: t.transpose(
                    pall[:, 1536 + 8 * dl:1544 + 8 * dl].bitcast(BF16),
                    adbf8[0:16, dl, :], ident_sb[0:16, 0:16]), "pe")
            TRDall = P.rec("pe")
            trd_hist[pi] = TRDall
            P.waits("vector", [TRDall])
            P.wait("vector", "scA")
            P.wait("vector", "scB")
            for dl in range(8):
                P.do("vector", lambda v, dl=dl: v.tensor_copy(
                    strip[:, dl, :],
                    pall[:, 1536 + 8 * dl:1544 + 8 * dl].bitcast(BF16)), "ve")
            strip_ev = P.rec("ve")
            P.waits("sync", [strip_ev])
            for dl in range(0, BAND + 1):
                if dl == 0:
                    out_ap = bass.AP(ab_scr.tensor, pi * (128 * 128 * 16),
                                     [[2064, 128], [1, 16]])
                    P.dma("sync", out_ap, strip[:, 0, :], scsem[pi])
                else:
                    out_ap = bass.AP(ab_scr.tensor,
                                     pi * (128 * 128 * 16) + dl * 16,
                                     [[2064, 128 - dl], [1, 16]])
                    P.dma("sync", out_ap, strip[dl:128, dl, :], scsem[pi])
            P.waits("sync", [ymm_hist[pi]])
            P.wait("sync", scsem[pi])
            P.dma("sync", ab2[pi],
                  ab_scr[pi].rearrange("(s x) -> s x", s=128), ltsem[pi])
            if stop_after == "p2b" and a == 0:
                P.wait("sync", ltsem[pi])
                tap("aband0", ab2[pi])
                return True

            P.wait("tensor", ltsem[pi])
            P.waits("tensor", [bank45_free if a > 0 else ("ve", 0)])
            abv = ab2[pi].rearrange("p (t h) -> p h t", h=16)
            for cc in range(4):
                ci = 4 * a + cc
                P.waits("tensor", [spp_ready[ci % 2]])
                for h in range(16):
                    P.pe_mm(lambda t, cc=cc, h=h, ci=ci: t.matmul(
                        pbank(4, 1024)[32 * cc:32 * cc + 32, 64 * h:64 * h + 64],
                        rst[64 * (h % 2):64 * (h % 2) + 64, h // 2, cc, :],
                        Sbf[64 * (h % 2):64 * (h % 2) + 64, ci % 2,
                            64 * (h // 2):64 * (h // 2) + 64],
                        start=True, stop=False,
                        tile_position=(64 * (h % 2), 32 * cc)),
                        key=("yst", h % 2), sem=None)
                    P.pe_mm(lambda t, cc=cc, h=h: t.matmul(
                        pbank(4, 1024)[32 * cc:32 * cc + 32, 64 * h:64 * h + 64],
                        A_sb[32 * cc:32 * cc + 32, 32 * h:32 * h + 32],
                        vt[32 * cc:32 * cc + 32, 64 * h:64 * h + 64],
                        start=False, stop=False,
                        tile_position=(32 * cc, 32 * cc)),
                        key=("y1", cc), sem=None)
                    P.pe_mm(lambda t, cc=cc, h=h: t.matmul(
                        pbank(4, 1024)[32 * cc:32 * cc + 32, 64 * h:64 * h + 64],
                        abv[:, h, 32 * cc:32 * cc + 32],
                        vt[:, 64 * h:64 * h + 64],
                        start=False, stop=True,
                        tile_position=(0, 32 * cc)),
                        key=("y2", 0), sem="pe")
                YST = P.rec("pe")
                P.waits("vector", [YST])
                dcolb = D_all[:, ci, :].unsqueeze(2).broadcast_to((128, 8, 64))
                P.do("vector", lambda v, ci=ci, dcolb=dcolb: v.tensor_mul(
                    Spp[:, (ci + 1) % 2].rearrange("p (q j) -> p q j", j=64),
                    Spp[:, ci % 2].rearrange("p (q j) -> p q j", j=64),
                    dcolb), "ve")
                P.do("vector", lambda v, ci=ci, cc=cc: v.tensor_add(
                    Spp[:, (ci + 1) % 2], Spp[:, (ci + 1) % 2],
                    m4[:, cc, :]), "ve")
                P.do("vector", lambda v, ci=ci: v.tensor_copy(
                    Sbf[:, (ci + 1) % 2], Spp[:, (ci + 1) % 2]), "ve")
                spp_ready[(ci + 1) % 2] = P.rec("ve")
            YMM = P.rec("pe")

            P.waits("vector", [YMM, OTM_prev])
            yv = pbank(4, 1024).rearrange("p (h j) -> p h j", j=64)
            P.do("vector", lambda v: v.reduce_sum(
                st16[:, :, 0], yv, axis=mybir.AxisListType.X), "ve")
            MU0 = P.rec("ve")
            P.waits("scalar", [YMM, X1_prev])
            P.do("scalar", lambda e: e.activation(
                ysq.rearrange("p (h j) -> p h j", j=64), yv, AF.Square), "sc")
            SQ = P.rec("sc")
            P.waits("vector", [SQ])
            P.do("vector", lambda v: v.reduce_sum(
                st16[:, :, 1], ysq.rearrange("p (h j) -> p h j", j=64),
                axis=mybir.AxisListType.X), "ve")
            P.do("vector", lambda v: v.tensor_scalar(
                st16[:, :, 0], st16[:, :, 0], 1.0 / 64, 0.0, ALU.mult,
                ALU.add), "ve")
            P.do("vector", lambda v: v.tensor_scalar(
                st16[:, :, 1], st16[:, :, 1], 1.0 / 64, 0.0, ALU.mult,
                ALU.add), "ve")
            P.do("vector", lambda v: v.tensor_mul(
                st16[:, :, 2], st16[:, :, 0], st16[:, :, 0]), "ve")
            P.do("vector", lambda v: v.tensor_sub(
                st16[:, :, 1], st16[:, :, 1], st16[:, :, 2]), "ve")
            VAR = P.rec("ve")
            P.waits("scalar", [VAR])
            P.do("scalar", lambda e: e.activation(
                st16[:, :, 3], st16[:, :, 1], AF.Ln, bias=eps_col), "sc")
            P.do("scalar", lambda e: e.activation(
                st16[:, :, 3], st16[:, :, 3], AF.Exp, scale=-0.5), "sc")
            RSTD = P.rec("sc")
            P.waits("vector", [RSTD])
            P.do("vector", lambda v: v.tensor_mul(
                rl_sb.rearrange("p (h j) -> p h j", j=64),
                st16[:, :, 3].unsqueeze(2).broadcast_to((128, 16, 64)),
                lnw_sb.rearrange("p (h j) -> p h j", j=64)), "ve")
            P.do("vector", lambda v: v.tensor_sub(
                yt_sb.rearrange("p (h j) -> p h j", j=64), yv,
                st16[:, :, 0].unsqueeze(2).broadcast_to((128, 16, 64))), "ve")
            YT = P.rec("ve")
            bank45_free = YT
            P.waits("scalar", [OTM_prev])
            P.do("scalar", lambda e, gt=gt: e.activation(sg_sb, gt, AF.Silu), "sc")
            SG = P.rec("sc")
            P.waits("gpsimd", [YT, SG])
            P.do("gpsimd", lambda g: g.tensor_mul(t2_sb, yt_sb, rl_sb), "gp")
            P.do("gpsimd", lambda g: g.tensor_add(t2_sb, t2_sb, lnb_sb), "gp")
            P.do("gpsimd", lambda g: g.tensor_mul(o_tm, t2_sb, sg_sb), "gp")
            OTM = P.rec("gp")
            if stop_after == "p2y" and a == 0:
                P.waits("sync", [OTM])
                tap("ytile0", yt_sb)
                return True

            P.waits("tensor", [OTM, ADB, strip_ev])
            for m in range(8):
                trb = 2 + (m % 2)
                if m >= 2:
                    P.waits("tensor", [ofm_ev[m - 2]])
                P.do("tensor", lambda t, m=m, trb=trb: t.transpose(
                    pbank_bf(trb, 128)[:, 0:128],
                    o_tm[:, 128 * m:128 * (m + 1)], ident_sb), "pe")
                TRO = P.rec("pe")
                P.waits("vector", [TRO])
                P.do("vector", lambda v, m=m, trb=trb: v.tensor_copy(
                    o_fm[:, m, :], pbank_bf(trb, 128)[:, 0:128]), "ve")
                ofm_ev[m] = P.rec("ve") if True else None
            OFM = P.rec("ve")
            OFM_prev = OFM

            P.waits("tensor", [OFM, oW_free if a > 0 else ("ve", 0)])
            P.wait("tensor", "gc")
            for n in range(2):
                for kk in range(8):
                    P.do("tensor", lambda t, n=n, kk=kk: t.matmul(
                        pbank(6 + n), o_fm[:, kk, :],
                        oWt_sb[:, kk, 512 * n:512 * (n + 1)],
                        start=(kk == 0), stop=(kk == 7)), "pe")
            OWM = P.rec("pe")
            P.waits("vector", [OWM, R1_prev])
            for n in range(2):
                P.do("vector", lambda v, n=n: v.tensor_add(
                    s1_sb[:, 512 * n:512 * (n + 1)], pbank(6 + n),
                    xt[:, 512 * n:512 * (n + 1)]), "ve")
            oW_free = P.rec("ve")
            P.do("vector", lambda v: v.reduce_sum(
                st16[:, 0, 0:1], s1_sb.rearrange("p (o d) -> p o d", o=1),
                axis=mybir.AxisListType.X), "ve")
            S1M = P.rec("ve")
            P.waits("scalar", [S1M])
            P.do("scalar", lambda e: e.activation(ysq, s1_sb, AF.Square), "sc")
            SQ1 = P.rec("sc")
            P.waits("vector", [SQ1])
            P.do("vector", lambda v: v.reduce_sum(
                st16[:, 1, 0:1], ysq.rearrange("p (o d) -> p o d", o=1),
                axis=mybir.AxisListType.X), "ve")
            P.do("vector", lambda v: v.tensor_scalar(
                st16[:, 0, 0:1], st16[:, 0, 0:1], 1.0 / D, 0.0, ALU.mult,
                ALU.add), "ve")
            P.do("vector", lambda v: v.tensor_scalar(
                st16[:, 1, 0:1], st16[:, 1, 0:1], 1.0 / D, 0.0, ALU.mult,
                ALU.add), "ve")
            P.do("vector", lambda v: v.tensor_mul(
                st16[:, 2, 0:1], st16[:, 0, 0:1], st16[:, 0, 0:1]), "ve")
            P.do("vector", lambda v: v.tensor_sub(
                st16[:, 1, 0:1], st16[:, 1, 0:1], st16[:, 2, 0:1]), "ve")
            V1 = P.rec("ve")
            P.waits("scalar", [V1])
            P.do("scalar", lambda e: e.activation(
                st16[:, 3, 0:1], st16[:, 1, 0:1], AF.Ln, bias=eps_col), "sc")
            P.do("scalar", lambda e: e.activation(
                st16[:, 3, 0:1], st16[:, 3, 0:1], AF.Exp, scale=-0.5), "sc")
            R1 = P.rec("sc")
            P.waits("vector", [R1])
            if x1st_prev:
                P.wait("vector", "st", x1st_prev)
            P.do("vector", lambda v: v.tensor_scalar(
                x1_sb, s1_sb, st16[:, 0, 0:1], st16[:, 3, 0:1],
                ALU.subtract, ALU.mult), "ve")
            X1 = P.rec("ve")
            P.waits("sync", [X1])
            P.dma("sync", x1_d[a], x1_sb, "st")
            x1st_prev = P.val("st")
            SUPD = P.rec("ve")
            tile_rel[pi] = [PBAND, PREPS, YMM, SG, oW_free, SUPD]
            ymm_hist[pi] = YMM
            ak2_hist[pi] = AK2
            pband_hist[pi] = PBAND
            red_hist[pi] = RED_prev
            OTM_prev, R1_prev, X1_prev = OTM, R1, X1
            if a == 0:
                tap("x1t0", x1_sb)
                tap("ytile0", yt_sb)
                tap("Asb0", A_sb)
                tap("aband0", ab2[pi])

        for a in range(NTILE):
            P.phase(f"p2t{a}")
            if emit_tile(a):
                _close(P, nc, ctx, [])
                return nc
        P.barrier()

        if _emit_ffn(P, nc, ar, pbank, pbank_bf, dict(locals())):
            _close(P, nc, ctx, [])
            return nc

        P.wait("sync", "outst")
        _close(P, nc, ctx, [])

    return nc


def _close(P, nc, ctx, _):
    def runner(eng):
        marks = {}
        for name, idxs in P.phases:
            marks[idxs[eng]] = name

        def run(e):
            cur = None
            sid = None
            for i, f in enumerate(P.ops[eng]):
                if i in marks:
                    if cur is not None:
                        nc.leave_named_scope(cur, sid, False)
                    cur = marks[i]
                    sid, _ = nc.enter_named_scope(cur, False)
                f(e)
            if cur is not None:
                nc.leave_named_scope(cur, sid, False)
        return run

    block = ctx.enter_context(nc.Block())
    block.sync(runner("sync"))
    block.gpsimd(runner("gpsimd"))
    block.vector(runner("vector"))
    block.scalar(runner("scalar"))
    block.tensor(runner("tensor"))


def _emit_pass2(P, nc, ar, pbank, pbank_bf, L):
    pass


def _emit_ffn(P, nc, ar, pbank, pbank_bf, L):
    x1_d, ha_d, hb_d, o3_d = L["x1_d"], L["ha_d"], L["hb_d"], L["o3_d"]
    w12t, h1tt, h2tt, w3tt = L["w12t"], L["h1tt"], L["h2tt"], L["w3tt"]
    out_hbm, ident_in = L["out_hbm"], L["ident_in"]
    scols_in, w3s_in = L["scols_in"], L["w3s_in"]
    tap, stop_after, ltsem = L["tap"], L["stop_after"], L["ltsem"]
    fwsem = {0: "fwA", 1: "fwB"}

    ar.reset(0)
    idf = ar.bf16(128)
    scol2 = ar.f32(4, 32)
    w3s2 = ar.f32(D)
    eps5 = ar.f32(1)
    FFNB = ar.mark()
    h_sb = ar.bf16(32, TC)
    FFN0 = ar.mark()
    P.dma("sync", idf, ident_in, "ldc")
    P.dma("sync", scol2, scols_in, "ldc")
    P.dma("sync", w3s2, w3s_in, "ldc")
    P.do("vector", lambda v: v.memset(eps5, EPS), "ve")
    P.wait("tensor", "ldc")
    P.wait("scalar", "ldc")
    P.wait("vector", "ldc")

    P.phase("ffnx")
    x1_fm = ar.bf16(8, TC)
    wf = [ar.bf16(16, 128), ar.bf16(16, 128)]
    sg1 = [ar.f32(512), ar.f32(512)]
    x1bb = [ar.bf16(D), ar.bf16(D)]
    TRlast = {0: ("pe", 0), 1: ("pe", 0)}
    ev_m = {}
    for a in range(NTILE):
        pi = a % 2
        P.waits("gpsimd", [TRlast[pi]])
        P.dma("gpsimd", x1bb[pi], x1_d[a], "gx")
        P.wait("tensor", "gx")
        for m in range(8):
            trb = 2 + (m % 2)
            if a * 8 + m >= 2:
                P.waits("tensor", [ev_m[a * 8 + m - 2]])
            P.do("tensor", lambda t, m=m, trb=trb, pi=pi: t.transpose(
                pbank_bf(trb, 128)[:, 0:128],
                x1bb[pi][:, 128 * m:128 * m + 128], idf), "pe")
            TR = P.rec("pe")
            P.waits("vector", [TR])
            P.do("vector", lambda v, m=m, trb=trb, a=a: v.tensor_copy(
                x1_fm[:, m, 128 * a:128 * a + 128],
                pbank_bf(trb, 128)[:, 0:128]), "ve")
            ev_m[a * 8 + m] = P.rec("ve")
        TRlast[pi] = P.rec("pe")
    X1FM = P.rec("ve")
    if stop_after == "ffnx":
        P.waits("sync", [X1FM])
        tap("x1fm0", x1_fm[:, 0, :])
        tap("x1fm7", x1_fm[:, 7, :])
        return True

    GMlast = {0: ("pe", 0), 1: ("pe", 0)}

    def w12_dma(oc, pi):
        P.dma("gpsimd", wf[pi].rearrange("p j o -> p (j o)"),
              w12t[oc], fwsem[pi])

    P.phase("ffn1")
    w12_dma(0, 0)
    ev_g = {0: ("ve", 0), 1: ("ve", 0)}
    sg_free = {0: ("ve", 0), 1: ("ve", 0)}
    for oc in range(32):
        pi = oc % 2
        if oc + 1 < 32:
            P.waits("gpsimd", [GMlast[(oc + 1) % 2]])
            w12_dma(oc + 1, (oc + 1) % 2)
        P.wait("tensor", fwsem[pi])
        if oc == 0:
            P.waits("tensor", [X1FM])
        for tp in range(2):
            g = (2 * oc + tp) % 2
            b0 = 4 * g
            P.waits("tensor", [ev_g[g]])
            for w in range(2):
                for tt in range(2):
                    tix = 2 * tp + tt
                    for kk in range(8):
                        P.pe_mm(lambda t, w=w, tt=tt, tix=tix, kk=kk, pi=pi,
                                b0=b0: t.matmul(
                            pbank(b0 + 2 * w + tt), wf[pi][:, 8 * w + kk, :],
                            x1_fm[:, kk, 512 * tix:512 * tix + 512],
                            start=(kk == 0), stop=(kk == 7)),
                            key=("f",), sem="pe")
            GM = P.rec("pe")
            GMlast[pi] = GM
            for tt in range(2):
                tix = 2 * tp + tt
                P.waits("scalar", [GM, sg_free[tt]])
                P.do("scalar", lambda e, b0=b0, tt=tt, oc=oc: e.activation(
                    sg1[tt], pbank(b0 + tt), AF.Silu,
                    scale=scol2[:, 0, oc:oc + 1]), "sc")
                SGr = P.rec("sc")
                P.waits("vector", [SGr])
                P.do("vector", lambda v, b0=b0, tt=tt, oc=oc, tix=tix:
                     v.scalar_tensor_tensor(
                         h_sb[:, oc, 512 * tix:512 * tix + 512],
                         pbank(b0 + 2 + tt), scol2[:, 1, oc:oc + 1],
                         sg1[tt], ALU.mult, ALU.mult), "ve")
                sg_free[tt] = P.rec("ve")
            ev_g[g] = P.rec("ve")
    if stop_after == "ffn1":
        tap("hs0", h_sb[:, 0, :])
        tap("hs31", h_sb[:, 31, :])
        return True

    P.barrier()
    ar.reset(FFN0)
    hw = [ar.bf16(32, 128), ar.bf16(32, 128)]
    hout = [ar.bf16(TC), ar.bf16(TC)]

    def emit_mid(wt_d, sidx, dst_d):
        MMl = {0: ("pe", 0), 1: ("pe", 0)}

        def hw_dma(oc, pi):
            P.dma("gpsimd", hw[pi].rearrange("p j o -> p (j o)"),
                  wt_d[oc], fwsem[pi])

        hw_dma(0, 0)
        evl = {0: ("sc", 0), 1: ("sc", 0)}
        stc = {0: 0, 1: 0}
        for oc in range(32):
            pi = oc % 2
            if oc + 1 < 32:
                P.waits("gpsimd", [MMl[(oc + 1) % 2]])
                hw_dma(oc + 1, (oc + 1) % 2)
            P.wait("tensor", fwsem[pi])
            P.waits("tensor", [evl[pi]])
            for t in range(4):
                for kk in range(32):
                    P.pe_mm(lambda t_, t=t, kk=kk, pi=pi: t_.matmul(
                        pbank(4 * pi + t), hw[pi][:, kk, :],
                        h_sb[:, kk, 512 * t:512 * t + 512],
                        start=(kk == 0), stop=(kk == 31)),
                        key=("f",), sem="pe")
            MM = P.rec("pe")
            MMl[pi] = MM
            P.waits("scalar", [MM])
            if stc[pi]:
                P.wait("scalar", "st", stc[pi])
            for t in range(4):
                P.do("scalar", lambda e, t=t, pi=pi, oc=oc, sidx=sidx:
                     e.activation(hout[pi][:, 512 * t:512 * t + 512],
                                  pbank(4 * pi + t), AF.Gelu,
                                  scale=scol2[:, sidx, oc:oc + 1]), "sc")
            EV = P.rec("sc")
            evl[pi] = EV
            P.waits("sync", [EV])
            P.dma("sync", dst_d[oc], hout[pi], "st")
            stc[pi] = P.val("st")

    def emit_reload(src_d):
        P.barrier()
        for c in range(32):
            if c % 2 == 0:
                P.dma("sync", h_sb[:, c, :], src_d[c], "ldc")
            else:
                P.dma("gpsimd", h_sb[:, c, :], src_d[c], "gc")
        P.wait("tensor", "ldc")
        P.wait("tensor", "gc")

    P.phase("ffn2")
    emit_mid(h1tt, 2, ha_d)
    emit_reload(ha_d)
    if stop_after == "ffn2":
        tap("ha0", h_sb[:, 0, :])
        tap("ha31", h_sb[:, 31, :])
        return True
    P.phase("ffn3")
    emit_mid(h2tt, 3, hb_d)
    emit_reload(hb_d)
    if stop_after == "ffn3":
        tap("hb0", h_sb[:, 0, :])
        tap("hb31", h_sb[:, 31, :])
        return True

    P.phase("ffn4")
    ar.reset(FFN0)
    w3q = [ar.bf16(32, 256), ar.bf16(32, 256)]
    o3e = [ar.f32(256), ar.f32(256)]
    MMc = {0: ("pe", 0), 1: ("pe", 0)}

    def w3_dma(c4, pi):
        P.dma("gpsimd", w3q[pi].rearrange("p j o -> p (j o)"),
              w3tt[c4], fwsem[pi])

    w3_dma(0, 0)
    evb = {0: ("ve", 0), 1: ("ve", 0)}
    sto = {0: 0, 1: 0}
    for c4 in range(4):
        pi = c4 % 2
        if c4 + 1 < 4:
            P.waits("gpsimd", [MMc[(c4 + 1) % 2]])
            w3_dma(c4 + 1, (c4 + 1) % 2)
        P.wait("tensor", fwsem[pi])
        for a in range(NTILE):
            pb = a % 2
            P.waits("tensor", [evb[pb]])
            for kk in range(32):
                P.pe_mm(lambda t, kk=kk, a=a, pi=pi, pb=pb: t.matmul(
                    pbank(pb, 256), h_sb[:, kk, 128 * a:128 * a + 128],
                    w3q[pi][:, kk, :],
                    start=(kk == 0), stop=(kk == 31)),
                    key=("w3",), sem="pe")
            MM = P.rec("pe")
            MMc[pi] = MM
            P.waits("vector", [MM])
            if sto[pb]:
                P.wait("vector", "st", sto[pb])
            P.do("vector", lambda v, pb=pb, c4=c4: v.tensor_mul(
                o3e[pb], pbank(pb, 256),
                w3s2[:, 256 * c4:256 * c4 + 256]), "ve")
            EVV = P.rec("ve")
            evb[pb] = EVV
            P.waits("sync", [EVV])
            P.dma("sync", o3_d[a][:, 256 * c4:256 * c4 + 256], o3e[pb], "st")
            sto[pb] = P.val("st")

    P.phase("ffn5")
    P.barrier()
    ar.reset(FFNB)
    x1r = ar.f32(8, D)
    o3r = ar.f32(8, D)
    s2b = ar.f32(8, D)
    ysq5 = ar.f32(8, D)
    st5 = ar.f32(4, 8)
    outb = ar.f32(8, D)
    ADDr = ("ve", 0)
    outc = 0
    for g in range(2):
        P.waits("sync", [ADDr])
        P.dma("sync", x1r,
              x1_d[8 * g:8 * g + 8].rearrange("j p d -> p j d"), ltsem[g % 2])
        P.waits("gpsimd", [ADDr])
        grec = P.dma("gpsimd", o3r,
                     o3_d[8 * g:8 * g + 8].rearrange("j p d -> p j d"), "gc")
        P.wait("vector", ltsem[g % 2])
        P.wait("vector", *grec)
        if outc:
            P.wait("vector", "outst", outc)
        P.do("vector", lambda v: v.tensor_add(s2b, x1r, o3r), "ve")
        ADDr = P.rec("ve")
        P.do("vector", lambda v: v.reduce_sum(
            st5[:, 0, 0:8], s2b, axis=mybir.AxisListType.X), "ve")
        S1M = P.rec("ve")
        P.waits("scalar", [S1M])
        P.do("scalar", lambda e: e.activation(ysq5, s2b, AF.Square), "sc")
        SQ = P.rec("sc")
        P.waits("vector", [SQ])
        P.do("vector", lambda v: v.reduce_sum(
            st5[:, 1, 0:8], ysq5, axis=mybir.AxisListType.X), "ve")
        P.do("vector", lambda v: v.tensor_scalar(
            st5[:, 0, 0:8], st5[:, 0, 0:8], 1.0 / D, 0.0, ALU.mult,
            ALU.add), "ve")
        P.do("vector", lambda v: v.tensor_scalar(
            st5[:, 1, 0:8], st5[:, 1, 0:8], 1.0 / D, 0.0, ALU.mult,
            ALU.add), "ve")
        P.do("vector", lambda v: v.tensor_mul(
            st5[:, 2, 0:8], st5[:, 0, 0:8], st5[:, 0, 0:8]), "ve")
        P.do("vector", lambda v: v.tensor_sub(
            st5[:, 1, 0:8], st5[:, 1, 0:8], st5[:, 2, 0:8]), "ve")
        V5 = P.rec("ve")
        P.waits("scalar", [V5])
        P.do("scalar", lambda e: e.activation(
            st5[:, 3, 0:8], st5[:, 1, 0:8], AF.Ln, bias=eps5), "sc")
        P.do("scalar", lambda e: e.activation(
            st5[:, 3, 0:8], st5[:, 3, 0:8], AF.Exp, scale=-0.5), "sc")
        R5 = P.rec("sc")
        P.waits("vector", [R5])
        P.do("vector", lambda v: v.tensor_sub(
            outb, s2b,
            st5[:, 0, 0:8].unsqueeze(2).broadcast_to((128, 8, D))), "ve")
        P.do("vector", lambda v: v.tensor_mul(
            outb, outb,
            st5[:, 3, 0:8].unsqueeze(2).broadcast_to((128, 8, D))), "ve")
        OUTr = P.rec("ve")
        P.waits("sync", [OUTr])
        P.dma("sync", out_hbm[1024 * g:1024 * (g + 1), :].rearrange(
            "(j p) d -> p j d", p=128), outb, "outst")
        outc = P.val("outst")
    return False



def host_prep(inputs):
    x = np.asarray(inputs["L0_attn_x"], np.float32).reshape(16384, D)
    rW = np.asarray(inputs["L0_attn_rW"], np.float32)
    kW = np.asarray(inputs["L0_attn_kW"], np.float32)
    vW = np.asarray(inputs["L0_attn_vW"], np.float32)
    gW = np.asarray(inputs["L0_attn_gW"], np.float32)
    oW = np.asarray(inputs["L0_attn_oW"], np.float32)
    wt_all = np.concatenate([rW.T, kW.T, vW.T, gW.T], axis=1)
    wtt = np.ascontiguousarray(
        wt_all.reshape(8, 128, 4, 1024).transpose(2, 1, 0, 3).reshape(
            4, 128, 8 * 1024))
    oWtt = np.ascontiguousarray(
        oW.T.reshape(8, 128, 1024).transpose(1, 0, 2).reshape(128, 8 * 1024))
    las, lbs, lls = [], [], []
    for p in ("r", "k", "v", "g", "d"):
        las.append(np.asarray(inputs[f"L0_attn_{p}_la"], np.float32))
        lbs.append(np.asarray(inputs[f"L0_attn_{p}_lb"], np.float32))
        lls.append(np.asarray(inputs[f"L0_attn_{p}_ll"], np.float32))
    laT = np.zeros((D, 384), np.float32)
    laT[:, :320] = np.concatenate(las, axis=0).T
    lbT = np.zeros((128, 3, D), np.float32)
    for pi in range(5):
        q, base = pi // 2, 64 * (pi % 2)
        lbT[base:base + 64, q, :] = lbs[pi].T
    ll_cols = np.ascontiguousarray(
        np.stack([v.reshape(8, 128).T for v in lls], axis=1))
    xw_col = np.ascontiguousarray(
        np.asarray(inputs["L0_attn_xw"], np.float32).reshape(8, 128).T)
    u_col = np.ascontiguousarray(
        np.asarray(inputs["L0_attn_u"], np.float32).reshape(8, 128).T)
    dll_col = np.ascontiguousarray(lls[4].reshape(8, 128).T)
    init_state = np.asarray(inputs["L0_attn_init_state"], np.float32)
    S0 = np.zeros((128, 512), np.float32)
    for h in range(H):
        S0[64 * (h % 2):64 * (h % 2) + 64,
           64 * (h // 2):64 * (h // 2) + 64] = init_state[h]
    ident = np.eye(128, dtype=BF)
    sM = np.zeros((32, 32), np.float32)
    si, ti = np.meshgrid(np.arange(32), np.arange(32), indexing="ij")
    sM[(ti - si) > BAND] = np.exp(QOFF)
    mscaleA = np.ascontiguousarray(np.tile(sM, (4, 16)))
    bmask = np.ones((128, SEG), np.float32)
    bmask[:, ::CH] = 0.0
    ind16 = np.zeros((128, 8, 16), BF)
    for m in range(8):
        ind16[0:64, m, 2 * m] = 1
        ind16[64:128, m, 2 * m + 1] = 1
    ln_w = np.asarray(inputs["L0_attn_ln_w"], np.float32)
    ln_b = np.asarray(inputs["L0_attn_ln_b"], np.float32)
    lnw_tile = np.ascontiguousarray(np.tile(ln_w, (128, 16)))
    lnb_tile = np.ascontiguousarray(np.tile(ln_b, (128, 16)))
    w3s_tile = np.ascontiguousarray(
        np.tile(np.asarray(inputs["L0_ffn_w3_s"], np.float32), (128, 1)))
    s_cols = np.ascontiguousarray(np.stack(
        [np.asarray(inputs[f"L0_ffn_{n}"], np.float32).reshape(32, 128).T
         for n in ("w1_s", "w2_s", "h1_s", "h2_s")], axis=1))
    def tile_oc(wT, kk, noc, o):
        K, O = wT.shape
        return np.ascontiguousarray(
            wT.reshape(kk, 128, noc, o).transpose(2, 1, 0, 3).reshape(
                noc, 128, kk * o))

    w1t = np.asarray(inputs["L0_ffn_w1_w"], np.float32).T
    w2t = np.asarray(inputs["L0_ffn_w2_w"], np.float32).T
    h1t = np.asarray(inputs["L0_ffn_h1_w"], np.float32).T
    h2t = np.asarray(inputs["L0_ffn_h2_w"], np.float32).T
    w3t = np.asarray(inputs["L0_ffn_w3_w"], np.float32).T
    w12t = np.ascontiguousarray(np.concatenate(
        [tile_oc(w1t, 8, 32, 128).reshape(32, 128, 8, 128),
         tile_oc(w2t, 8, 32, 128).reshape(32, 128, 8, 128)],
        axis=2).reshape(32, 128, 16 * 128))
    h1tt = tile_oc(h1t, 32, 32, 128)
    h2tt = tile_oc(h2t, 32, 32, 128)
    w3tt = tile_oc(w3t, 32, 4, 256)

    shared = dict(wtt=wtt, oWtt=oWtt, laT=laT, lbT=lbT, ll_cols=ll_cols,
                  xw_col=xw_col, dll_col=dll_col, u_col=u_col, S0=S0,
                  ident=ident, mscaleA=mscaleA, bmask=bmask, ind16=ind16,
                  lnw_tile=lnw_tile, lnb_tile=lnb_tile, w3s_tile=w3s_tile,
                  s_cols=s_cols, w12t=w12t, h1tt=h1tt, h2tt=h2tt, w3tt=w3tt)
    in_maps = []
    for c in range(NCORE):
        im = dict(shared)
        im["x_hbm"] = np.ascontiguousarray(x[c * TC:(c + 1) * TC])
        im["xprev"] = (np.zeros((1, D), np.float32) if c == 0
                       else np.ascontiguousarray(x[c * TC - 1:c * TC]))
        pmv = np.zeros((128, 8), np.float32)
        pmv[:, :c] = 1.0
        im["pm"] = pmv
        im["pm1"] = np.ascontiguousarray(1.0 - pmv)
        in_maps.append(im)
    return in_maps


def kernel(**inputs):
    in_maps = host_prep(inputs)
    nc = build()
    res = run_bass_kernel_spmd(nc, in_maps, core_ids=list(range(NCORE)))
    out = np.concatenate([res.results[i]["out"] for i in range(NCORE)], axis=0)
    return out.reshape(1, 16384, 1024)
